# version 06
# baseline (speedup 1.0000x reference)
import numpy as np
import ml_dtypes
import concourse.bass as bass
import concourse.mybir as mybir
from concourse.bass_core_rust import IndirectOffsetOnAxis
from concourse.bass_utils import run_bass_kernel_spmd

F32 = mybir.dt.float32
BF16 = mybir.dt.bfloat16
U32 = mybir.dt.uint32
OP = mybir.AluOpType
ACT = mybir.ActivationFunctionType
AXX = mybir.AxisListType.X

S = 16384
D = 1024
HID = 4096
NE = 8
NCORE = 8
SOWN = 2048
NT = 16
CAP = 2458
PAD = 448
NSLOT = NE * PAD
KB = 512
NKB = S // KB
NITER = 40
EPS = 1e-5
SG = 256
NSG = NSLOT // SG
KI = 1024
BIG = 1.0e6


def build_nc():
    nc = bass.Bass("TRN2", target_bir_lowering=False)
    ap = lambda t: t.ap()

    q_own = ap(nc.dram_tensor("q_own", [SOWN, D], F32, kind="ExternalInput"))
    q_res = ap(nc.dram_tensor("q_res", [2, KI, D], F32, kind="ExternalInput"))
    wq = ap(nc.dram_tensor("wq", [D, D], F32, kind="ExternalInput"))
    wk = ap(nc.dram_tensor("wk", [D, D], F32, kind="ExternalInput"))
    wv = ap(nc.dram_tensor("wv", [D, D], F32, kind="ExternalInput"))
    wo = ap(nc.dram_tensor("wo", [D, D], F32, kind="ExternalInput"))
    wr = ap(nc.dram_tensor("wr", [D, NE], F32, kind="ExternalInput"))
    brr = ap(nc.dram_tensor("brr", [128, NE], F32, kind="ExternalInput"))
    w1i = ap(nc.dram_tensor("w1i", [D, HID], F32, kind="ExternalInput"))
    b1i = ap(nc.dram_tensor("b1i", [128, HID // 128], F32, kind="ExternalInput"))
    w2i = ap(nc.dram_tensor("w2i", [HID, D], F32, kind="ExternalInput"))
    b2i = ap(nc.dram_tensor("b2i", [128, D], F32, kind="ExternalInput"))
    cvec = ap(nc.dram_tensor("cvec", [128, 4], F32, kind="ExternalInput"))
    out = ap(nc.dram_tensor("out", [2, KI, D], F32, kind="ExternalOutput"))

    idnb = ap(nc.inline_tensor(np.eye(128, dtype=np.float32).astype(ml_dtypes.bfloat16),
                               "c_idnb"))
    ones_c = ap(nc.inline_tensor(np.ones((128, 128), np.float32), "c_ones"))
    ltri_c = ap(nc.inline_tensor(
        (np.arange(128)[:, None] < np.arange(128)[None, :]).astype(np.float32), "c_ltri"))
    io8 = np.broadcast_to(np.arange(8, dtype=np.float32), (128, NT, 8)).reshape(128, 128)
    iota8_c = ap(nc.inline_tensor(io8.copy(), "c_iota8"))
    iota8m_c = ap(nc.inline_tensor((io8 - 8.0).copy(), "c_iota8m"))
    onesr_c = ap(nc.inline_tensor(np.ones((128, NT), np.float32), "c_onesr"))
    sglob = np.arange(NSLOT).reshape(NSLOT // 128, 128)
    fidx_i = ap(nc.inline_tensor((sglob // PAD * NSLOT).T.astype(np.float32).copy(),
                                 "c_fidxi"))
    fidx_r = ap(nc.inline_tensor((sglob % PAD).T.astype(np.float32).copy(), "c_fidxr"))
    pio_c = ap(nc.inline_tensor(np.arange(128, dtype=np.float32).reshape(128, 1).copy(),
                                "c_pio"))

    kv_k_loc = ap(nc.dram_tensor("kv_k_loc", [D, SOWN], BF16))
    kv_v_loc = ap(nc.dram_tensor("kv_v_loc", [SOWN, D], BF16))
    k_all = ap(nc.dram_tensor("k_all", [NCORE, D, SOWN], BF16))
    v_all = ap(nc.dram_tensor("v_all", [S, D], BF16))
    gates_loc = ap(nc.dram_tensor("gates_loc", [128, 64], F32))
    gates_all = ap(nc.dram_tensor("gates_all", [NCORE, 128, 64], F32))
    disp_loc = ap(nc.dram_tensor("disp_loc", [NSLOT + 32 * 128, D], BF16))
    disp_all = ap(nc.dram_tensor("disp_all", [NCORE * NSLOT, D], BF16))
    ret_loc = ap(nc.dram_tensor("ret_loc", [NSLOT, D], BF16))
    ret_all = ap(nc.dram_tensor("ret_all", [NCORE * NSLOT + 128, D], BF16))
    x1_spill = ap(nc.dram_tensor("x1_spill", [NT, 128, D], F32))

    K = 1024
    SB0 = 16512
    AT = lambda name, shape, dt, o: nc.alloc_sbuf_tensor_at(name, shape, dt,
                                                            offset=SB0 + o)
    qT = AT("qT", [128, 8, SOWN], BF16, 0)
    Ksb = AT("Ksb", [128, 8, SOWN], BF16, 32 * K)
    Osb = AT("Osb", [128, NT, D], F32, 0)
    w2s = AT("w2s", [128, 32, D], BF16, 0)
    Qf = AT("Qf", [128, 8, SOWN], BF16, 64 * K)
    zdisp = AT("zdisp", [128, NSLOT // 128, D], BF16, 64 * K)
    Vsb = AT("Vsb", [128, NT, D], BF16, 96 * K)
    Obf = AT("Obf", [128, NT, D], BF16, 96 * K)
    w1s = AT("w1s", [128, 8, HID], BF16, 64 * K)
    x1bf = AT("x1bf", [128, NT, D], BF16, 128 * K)
    hsb = AT("hsb", [128, 32, SG], BF16, 128 * K)
    xdT = AT("xdT", [128, 8, SG], BF16, 144 * K)
    xdisp = AT("xdisp", [128, 2, SG // 128, D], BF16, 152 * K)
    xdTC = AT("xdTC", [128, 8, 128], BF16, 0)
    cjunk = AT("cjunk", [128, D], F32, 4 * K)
    x1sc = AT("x1sc", [128, D], F32, 8 * K)
    x1lo = AT("x1lo", [128, D], BF16, 12 * K)
    xdTL = AT("xdTL", [128, 8, 128], BF16, 14 * K)
    wT = AT("wT", [128, 8, D], BF16, 160 * K)
    kbuf = AT("kbuf", [128, 2, 8, KB], BF16, 160 * K)
    orow = AT("orow", [128, D], F32, 160 * K)
    orowb = AT("orowb", [128, D], BF16, 164 * K)
    ga0 = AT("ga0", [128, D], BF16, 166 * K)
    ga1 = AT("ga1", [128, D], BF16, 168 * K)
    yrow = AT("yrow", [128, D], F32, 170 * K)
    x1rl = AT("x1rl", [128, D], F32, 174 * K)
    b2s = AT("b2s", [128, D], F32, 178 * K)
    vbuf = AT("vbuf", [128, 2, 4, D], BF16, 176 * K)
    wstage = AT("wstage", [128, 2, D], F32, 176 * K)
    wstgb = AT("wstgb", [128, 2, D], BF16, 184 * K)
    qres_b = AT("qres_b", [128, 2, D], F32, 176 * K)
    xstg = AT("xstg", [128, D], F32, 184 * K)
    off = [192 * K]

    def perm(name, shape, dt):
        t = AT(name, shape, dt, off[0])
        n = int(np.prod(shape[1:])) * mybir.dt.size(dt)
        off[0] += (n + 31) & ~31
        return t

    idn_s = perm("idn_s", [128, 128], BF16)
    ones_s = perm("ones_s", [128, 128], F32)
    ltri_s = perm("ltri_s", [128, 128], F32)
    iota8_s = perm("iota8_s", [128, NT, 8], F32)
    iota8m_s = perm("iota8m_s", [128, NT, 8], F32)
    onesr_s = perm("onesr_s", [128, NT], F32)
    fii_s = perm("fii_s", [128, NSLOT // 128], F32)
    fir_s = perm("fir_s", [128, NSLOT // 128], F32)
    fidxf_s = perm("fidxf_s", [128, NSLOT // 128], F32)
    fidx_s = perm("fidx_s", [128, NSLOT // 128], U32)
    cv = perm("cv", [128, 4], F32)
    wrs = perm("wrs", [128, 8, NE], BF16)
    brs = perm("brs", [128, NE], F32)
    b1s = perm("b1s", [128, HID // 128], F32)
    mst = perm("mst", [128, NT], F32)
    lsum = perm("lsum", [128, NT], F32)
    nmn = perm("nmn", [128, 2], F32)
    dmm = perm("dmm", [128, 2], F32)
    rmx = perm("rmx", [128, 1], F32)
    alp = perm("alp", [128, 2], F32)
    lbb = perm("lbb", [128, 2], F32)
    psb = perm("psb", [128, KB], BF16)
    ptb = perm("ptb", [128, KB], BF16)
    oinv = perm("oinv", [128, 1], F32)
    probs = perm("probs", [128, NT, NE], F32)
    scr8 = perm("scr8", [128, NT, NE], F32)
    scr8b = perm("scr8b", [128, NT, NE], F32)
    g1v = perm("g1v", [128, NT], F32)
    g2v = perm("g2v", [128, NT], F32)
    e1v = perm("e1v", [128, NT], F32)
    e2v = perm("e2v", [128, NT], F32)
    gall = perm("gall", [128, 8, 64], F32)
    mete = perm("mete", [128, 8, 8, 32], BF16)
    junkc = perm("junkc", [128, 256], BF16)
    thl = perm("thl", [128, NE], F32)
    thh = perm("thh", [128, NE], F32)
    thm = perm("thm", [128, NE], F32)
    cntp = perm("cntp", [128, NE], F32)
    cm1 = perm("cm1", [128, NE], F32)
    cm2 = perm("cm2", [128, NE], F32)
    ctt = perm("ctt", [128, NE], F32)
    thsel = perm("thsel", [128, NT], F32)
    vld = perm("vld", [128, 2, NT], F32)
    rnk = perm("rnk", [128, 2, NT], F32)
    rs8 = perm("rs8", [128, 2, NE], F32)
    off8 = perm("off8", [128, 2, NE], F32)
    tot8 = perm("tot8", [128, NE], F32)
    asc = perm("asc", [128, NT], F32)
    asc2 = perm("asc2", [128, NT], F32)
    asc3 = perm("asc3", [128, NT], F32)
    didxf = perm("didxf", [128, 2, NT], F32)
    didx = perm("didx", [128, 2, NT], U32)
    gidx = perm("gidx", [128, 2, NT], U32)
    stat = perm("stat", [128, 4], F32)
    lgts = perm("lgts", [128, NE], F32)
    epss = perm("epss", [128, 1], F32)
    wrf = perm("wrf", [128, NE, NE], F32)
    wrl = perm("wrl", [128, NE, NE], BF16)
    pio_s = perm("pio_s", [128, 1], F32)
    zrow_d = perm("zrow_d", [128, 1], F32)
    zrow_g = perm("zrow_g", [128, 1], F32)
    tio16 = perm("tio16", [128, NT], F32)
    zt16 = perm("zt16", [128, NT], F32)
    assert SB0 + off[0] <= 229344, (SB0, off[0])

    ps_s = nc.alloc_psum_tensor("ps_s", [128, KB], F32)
    ps_s2 = nc.alloc_psum_tensor("ps_s2", [128, KB], F32)
    ps_pt = nc.alloc_psum_tensor("ps_pt", [128, KB], BF16)
    ps_pv = nc.alloc_psum_tensor("ps_pv", [128, 2, KB], F32)
    ps_m = nc.alloc_psum_tensor("ps_m", [128, 2, KB], F32)
    ps_t = nc.alloc_psum_tensor("ps_t", [128, 128], F32)

    sd = nc.alloc_semaphore("sd")
    sdc = nc.alloc_semaphore("sdc")
    sdl0 = nc.alloc_semaphore("sdl0")
    sdl1 = nc.alloc_semaphore("sdl1")
    sdk0 = nc.alloc_semaphore("sdk0")
    sdk1 = nc.alloc_semaphore("sdk1")
    sq0 = nc.alloc_semaphore("sq0")
    sq1 = nc.alloc_semaphore("sq1")
    dg = nc.alloc_semaphore("dg")
    dgx = nc.alloc_semaphore("dgx")
    dgz = nc.alloc_semaphore("dgz")
    dgx2a = nc.alloc_semaphore("dgx2a")
    dgx2b = nc.alloc_semaphore("dgx2b")
    dro = nc.alloc_semaphore("dro")
    dgg = nc.alloc_semaphore("dgg")
    dgo = nc.alloc_semaphore("dgo")
    cc = nc.alloc_semaphore("cc")
    vs = nc.alloc_semaphore("vs")
    s2 = nc.alloc_semaphore("s2")
    tsm = nc.alloc_semaphore("ts")
    with nc.Block() as block:
        sdl = [sdl0, sdl1]
        sdk = [sdk0, sdk1]
        sq = [sq0, sq1]
        dgx2 = [dgx2a, dgx2b]
        C = {"sd": 0, "dg": 0, "cc": 0, "vs": 0, "s2": 0, "ts": 0}
        POOL = mybir.EngineType.Pool
        _rind = [nc.alloc_register(POOL, "rind%d" % i) for i in range(3)]
        for _r in _rind:
            nc.free_register(_r)

        def _rfree():
            for _r in _rind:
                nc.free_register(_r)

        @block.sync
        def _(sy):
            sy.dma_start(out=idn_s[:, :], in_=idnb[:, :]).then_inc(sdc, 16)
            sy.dma_start(out=ones_s[:, :], in_=ones_c[:, :]).then_inc(sdc, 16)
            sy.dma_start(out=ltri_s[:, :], in_=ltri_c[:, :]).then_inc(sdc, 16)
            sy.dma_start(out=iota8_s.rearrange("p a e -> p (a e)")[:, :],
                         in_=iota8_c[:, :]).then_inc(sdc, 16)
            sy.dma_start(out=iota8m_s.rearrange("p a e -> p (a e)")[:, :],
                         in_=iota8m_c[:, :]).then_inc(sdc, 16)
            sy.dma_start(out=onesr_s[:, :], in_=onesr_c[:, :]).then_inc(sdc, 16)
            sy.dma_start(out=fii_s[:, :], in_=fidx_i[:, :]).then_inc(sdc, 16)
            sy.dma_start(out=fir_s[:, :], in_=fidx_r[:, :]).then_inc(sdc, 16)
            sy.dma_start(out=cv[:, :], in_=cvec[:, :]).then_inc(sdc, 16)
            sy.dma_start(out=brs[:, :], in_=brr[:, :]).then_inc(sdc, 16)
            sy.dma_start(out=b1s[:, :], in_=b1i[:, :]).then_inc(sdc, 16)
            sy.dma_start(out=wrf[:, :, :],
                         in_=wr.rearrange("(dc p) e -> p dc e", p=128)[:, :, :]
                         ).then_inc(sdc, 16)
            sy.dma_start(out=pio_s[:, :], in_=pio_c[:, :]).then_inc(sdc, 16)
        NCONST = 13 * 16


        LT = {"n": [0, 0]}

        def load_T(src, ntile, dest_fn, gate=None):
            b_s2, b_ts, b_vs = C["s2"], C["ts"], C["vs"]
            b_lt = list(LT["n"])

            @block.sync
            def _(sy):
                if gate is not None:
                    sy.wait_ge(gate[0], gate[1])
                sy.wait_ge(dg, 16)
                sy.wait_ge(s2, b_s2)
                for t in range(ntile):
                    if t >= 2:
                        sy.wait_ge(s2, b_s2 + t - 1)
                    sy.dma_start(out=wstage[:, t % 2, :],
                                 in_=src[t * 128:(t + 1) * 128, :]
                                 ).then_inc(sdl[t % 2], 16)

            @block.scalar
            def _(sc):
                sc.wait_ge(tsm, b_ts)
                for t in range(ntile):
                    sc.wait_ge(sdl[t % 2], (b_lt[t % 2] + t // 2 + 1) * 16)
                    if t >= 2:
                        sc.wait_ge(tsm, b_ts + (t - 1) * 8)
                    sc.activation(wstgb[:, t % 2, :], wstage[:, t % 2, :], ACT.Copy,
                                  bias=0.0, scale=1.0).then_inc(s2, 1)
            LT["n"][0] += (ntile + 1) // 2
            LT["n"][1] += ntile // 2

            @block.tensor
            def _(te):
                te.wait_ge(sdc, NCONST)
                te.wait_ge(vs, b_vs)
                for t in range(ntile):
                    te.wait_ge(s2, b_s2 + t + 1)
                    for dc in range(8):
                        i = t * 8 + dc
                        if i >= 2:
                            te.wait_ge(vs, b_vs + i - 1)
                        te.transpose(ps_pt[:, (i % 2) * 128:(i % 2) * 128 + 128],
                                     wstgb[:, t % 2, dc * 128:(dc + 1) * 128],
                                     idn_s[:, :]).then_inc(tsm, 1)

            @block.vector
            def _(ve):
                for t in range(ntile):
                    for dc in range(8):
                        i = t * 8 + dc
                        ve.wait_ge(tsm, b_ts + i + 1)
                        ve.tensor_copy(dest_fn(t, dc),
                                       ps_pt[:, (i % 2) * 128:(i % 2) * 128 + 128])
                        ve.drain()
                        ve.engine_nop().then_inc(vs, 1)

            C["s2"] += ntile
            C["ts"] += ntile * 8
            C["vs"] += ntile * 8

        @block.vector
        def _(ve):
            ve.memset(zdisp[:, :, :], 0.0)
            ve.drain()
            ve.engine_nop().then_inc(vs, 1)
        C["vs"] += 1

        @block.gpsimd
        def _(g):
            g.wait_ge(vs, C["vs"])
            for zc in range(4):
                g.dma_start(out=disp_loc.rearrange("(a p) d -> p a d", p=128)[:, zc * 7:(zc + 1) * 7, :],
                            in_=zdisp[:, zc * 7:(zc + 1) * 7, :]).then_inc(dgz, 16)
            g.dma_start(out=ret_all[NCORE * NSLOT:NCORE * NSLOT + 128, :],
                        in_=zdisp[:, 0, :]).then_inc(dgz, 16)
            g.dma_start(out=wrs[:, :, :],
                        in_=wr.rearrange("(dc p) e -> p dc e", p=128)[:, :, :]
                        ).then_inc(dg, 16)
        DGZ = 80
        C["dg"] += 16

        load_T(q_own, NT, lambda t, dc: qT[:, dc, t * 128:(t + 1) * 128],
               gate=(dgz, DGZ))

        def proj(wsrc, kind):
            load_T(wsrc, 8, lambda t, dc: wT[:, dc, t * 128:(t + 1) * 128])
            b_ts, b_s2, b_vs = C["ts"], C["s2"], C["vs"]
            if kind in ("K", "Q"):
                dest = Ksb if kind == "K" else Qf
                scl = 0.125 if kind == "Q" else 1.0

                @block.tensor
                def _(te):
                    te.wait_ge(vs, b_vs)
                    n = 0
                    for ec in range(8):
                        for sh in range(4):
                            n += 1
                            if n > 2:
                                te.wait_ge(s2, b_s2 + n - 2)
                            for dc in range(8):
                                _mi = te.matmul(ps_m[:, n % 2, :],
                                                wT[:, dc, ec * 128:(ec + 1) * 128],
                                                qT[:, dc, sh * 512:(sh + 1) * 512],
                                                start=(dc == 0), stop=(dc == 7))
                            _mi.then_inc(tsm, 1)

                @block.scalar
                def _(sc):
                    if kind == "Q":
                        sc.nop()
                    n = 0
                    for ec in range(8):
                        for sh in range(4):
                            n += 1
                            sc.wait_ge(tsm, b_ts + n)
                            sc.activation(dest[:, ec, sh * 512:(sh + 1) * 512],
                                          ps_m[:, n % 2, :], ACT.Copy, bias=0.0,
                                          scale=scl).then_inc(s2, 1)
            else:
                @block.tensor
                def _(te):
                    te.wait_ge(vs, b_vs)
                    n = 0
                    for st in range(NT):
                        for eh in range(2):
                            n += 1
                            if n > 2:
                                te.wait_ge(s2, b_s2 + n - 2)
                            for dc in range(8):
                                _mi = te.matmul(ps_m[:, n % 2, :],
                                                qT[:, dc, st * 128:(st + 1) * 128],
                                                wT[:, dc, eh * 512:(eh + 1) * 512],
                                                start=(dc == 0), stop=(dc == 7))
                            _mi.then_inc(tsm, 1)

                @block.scalar
                def _(sc):
                    sc.nop()
                    n = 0
                    for st in range(NT):
                        for eh in range(2):
                            n += 1
                            sc.wait_ge(tsm, b_ts + n)
                            sc.activation(Vsb[:, st, eh * 512:(eh + 1) * 512],
                                          ps_m[:, n % 2, :], ACT.Copy, bias=0.0,
                                          scale=1.0).then_inc(s2, 1)
            C["ts"] += 32
            C["s2"] += 32

        proj(wk, "K")
        proj(wv, "V")
        proj(wq, "Q")

        dg_a3 = C["dg"]

        @block.gpsimd
        def _(g):
            g.wait_ge(s2, C["s2"])
            for zc in range(2):
                g.dma_start(out=kv_k_loc.rearrange("(c p) s -> p c s", p=128)[:, zc * 4:(zc + 1) * 4, :],
                            in_=Ksb[:, zc * 4:(zc + 1) * 4, :]).then_inc(dg, 16)
                g.dma_start(out=kv_v_loc.rearrange("(t p) e -> p t e", p=128)[:, zc * 8:(zc + 1) * 8, :],
                            in_=Vsb[:, zc * 8:(zc + 1) * 8, :]).then_inc(dg, 16)
            g.wait_ge(dg, dg_a3 + 64)
            g.collective_compute("AllGather", OP.bypass,
                                 replica_groups=[list(range(NCORE))],
                                 ins=[kv_k_loc[:, :]], outs=[k_all[:, :, :]]
                                 ).then_inc(cc, 1)
            g.collective_compute("AllGather", OP.bypass,
                                 replica_groups=[list(range(NCORE))],
                                 ins=[kv_v_loc[:, :]], outs=[v_all[:, :]]
                                 ).then_inc(cc, 1)
        C["dg"] += 64
        C["cc"] += 2

        @block.vector
        def _(ve):
            ve.wait_ge(tsm, C["ts"])
            ve.wait_ge(dg, dg_a3 + 64)
            ve.memset(mst[:, :], -3.0e38)
            ve.memset(lsum[:, :], 0.0)
            ve.memset(epss[:, :], EPS)
            ve.memset(Osb[:, :, :], 0.0)
            ve.drain()
            ve.engine_nop().then_inc(vs, 1)
        C["vs"] += 1

        b_ts, b_vs, b_s2 = C["ts"], C["vs"], C["s2"]
        NIT = NKB * NT

        @block.sync
        def _(sy):
            sy.wait_ge(cc, C["cc"])
            for kb in range(NKB):
                if kb >= 2:
                    sy.wait_ge(tsm, b_ts + 3 * NT * (kb - 1) + 1)
                r, s0 = kb // 4, (kb % 4) * KB
                sy.dma_start(out=kbuf[:, kb % 2, :, :],
                             in_=k_all[r, :, s0:s0 + KB].rearrange(
                                 "(c p) s -> p c s", p=128)[:, :, :]
                             ).then_inc(sdk[kb % 2], 16)
                sy.dma_start(out=vbuf[:, kb % 2, :, :],
                             in_=v_all[kb * KB:(kb + 1) * KB, :].rearrange(
                                 "(c p) e -> p c e", p=128)[:, :, :]
                             ).then_inc(sdk[kb % 2], 16)

        def s_cum(j):
            return 1 if j == 0 else 3 * j - 1

        def t_cum(it):
            return 3 * it + 3 if it < NIT - 1 else 3 * it + 2

        def pv_cum(it):
            return 3 * it + 4 if it < NIT - 1 else 3 * it + 3

        @block.tensor
        def _(te):
            def sgrp(j):
                kbj, qtj = j // NT, j % NT
                sN = ps_s if j % 2 == 0 else ps_s2
                if qtj == 0:
                    te.wait_ge(sdk[kbj % 2], (kbj // 2 + 1) * 32)
                if j >= 2:
                    te.wait_ge(s2, b_s2 + 3 * (j - 2) + 1)
                for ec in range(8):
                    _mi = te.matmul(sN[:, :], Qf[:, ec, qtj * 128:(qtj + 1) * 128],
                                    kbuf[:, kbj % 2, ec, :],
                                    start=(ec == 0), stop=(ec == 7))
                _mi.then_inc(tsm, 1)
            sgrp(0)
            for kb in range(NKB):
                for qt in range(NT):
                    it = kb * NT + qt
                    if it + 1 < NIT:
                        sgrp(it + 1)
                    te.wait_ge(s2, b_s2 + 3 * it + 1)
                    for kc in range(4):
                        _mi = te.transpose(ps_pt[:, kc * 128:(kc + 1) * 128],
                                           psb[:, kc * 128:(kc + 1) * 128], idn_s[:, :])
                    _mi.then_inc(tsm, 1)
                    te.wait_ge(s2, b_s2 + 3 * it + 3)
                    if it >= 1:
                        te.wait_ge(vs, b_vs + 2 * it)
                    for kc in range(4):
                        for eh in range(2):
                            _mi = te.matmul(ps_pv[:, eh, :],
                                            ptb[:, kc * 128:(kc + 1) * 128],
                                            vbuf[:, kb % 2, kc, eh * 512:(eh + 1) * 512],
                                            start=(kc == 0), stop=(kc == 3))
                    _mi.then_inc(tsm, 1)
        C["ts"] += NIT * 3

        @block.vector
        def _(ve):
            for kb in range(NKB):
                for qt in range(NT):
                    it = kb * NT + qt
                    sN = ps_s if it % 2 == 0 else ps_s2
                    ve.wait_ge(tsm, b_ts + s_cum(it))
                    if it >= 2:
                        ve.wait_ge(s2, b_s2 + 3 * (it - 2) + 2)
                    ve.tensor_reduce(rmx[:, :], sN[:, :], AXX, OP.max)
                    ve.drain()
                    ve.tensor_max(rmx[:, :], rmx[:, :], mst[:, qt:qt + 1])
                    ve.drain()
                    ve.tensor_scalar(nmn[:, it % 2:it % 2 + 1], rmx[:, :], -1.0, None,
                                     OP.mult)
                    ve.tensor_sub(dmm[:, it % 2:it % 2 + 1], mst[:, qt:qt + 1],
                                  rmx[:, :])
                    ve.drain()
                    ve.tensor_copy(mst[:, qt:qt + 1], rmx[:, :])
                    ve.drain()
                    ve.engine_nop().then_inc(vs, 1)
                    ve.wait_ge(tsm, b_ts + pv_cum(it))
                    ve.wait_ge(s2, b_s2 + 3 * it + 2)
                    for eh in range(2):
                        ve.scalar_tensor_tensor(
                            Osb[:, qt, eh * 512:(eh + 1) * 512],
                            Osb[:, qt, eh * 512:(eh + 1) * 512],
                            alp[:, it % 2:it % 2 + 1], ps_pv[:, eh, :],
                            OP.mult, OP.add)
                    ve.drain()
                    ve.scalar_tensor_tensor(lsum[:, qt:qt + 1], lsum[:, qt:qt + 1],
                                            alp[:, it % 2:it % 2 + 1],
                                            lbb[:, it % 2:it % 2 + 1], OP.mult, OP.add)
                    ve.drain()
                    ve.engine_nop().then_inc(vs, 1)
        C["vs"] += NIT * 2

        @block.scalar
        def _(sc):
            for kb in range(NKB):
                for qt in range(NT):
                    it = kb * NT + qt
                    sN = ps_s if it % 2 == 0 else ps_s2
                    sc.wait_ge(vs, b_vs + 2 * it + 1)
                    if it >= 2:
                        sc.wait_ge(vs, b_vs + 2 * (it - 2) + 2)
                    if it >= 1:
                        sc.wait_ge(tsm, b_ts + t_cum(it - 1))
                    sc.activation(psb[:, :], sN[:, :], ACT.Exp,
                                  bias=nmn[:, it % 2:it % 2 + 1], scale=1.0,
                                  accum_out=lbb[:, it % 2:it % 2 + 1]).then_inc(s2, 1)
                    sc.activation(alp[:, it % 2:it % 2 + 1], dmm[:, it % 2:it % 2 + 1],
                                  ACT.Exp, bias=0.0, scale=1.0).then_inc(s2, 1)
                    sc.wait_ge(tsm, b_ts + t_cum(it))
                    sc.activation(ptb[:, :], ps_pt[:, :], ACT.Copy, bias=0.0,
                                  scale=1.0).then_inc(s2, 1)
        C["s2"] += NIT * 3

        @block.vector
        def _(ve):
            for t in range(NT):
                ve.reciprocal(oinv[:, :], lsum[:, t:t + 1])
                ve.drain()
                ve.tensor_scalar(Obf[:, t, :], Osb[:, t, :], oinv[:, 0:1], None, OP.mult)
                ve.drain()
            ve.engine_nop().then_inc(vs, 1)
        C["vs"] += 1
        vs_obf = C["vs"]

        load_T(wo, 8, lambda t, dc: wT[:, dc, t * 128:(t + 1) * 128],
               gate=(vs, vs_obf))
        c_ts, c_vs, c_s2 = C["ts"], C["vs"], C["s2"]

        @block.sync
        def _(sy):
            sy.wait_ge(s2, c_s2)
            for t in range(NT):
                j, m = t // 8, t % 8
                if t >= 2:
                    sy.wait_ge(vs, c_vs + (t - 2) * 6 + 1)
                sy.dma_start(out=qres_b[:, t % 2, :],
                             in_=q_res[j, m * 128:(m + 1) * 128, :]
                             ).then_inc(sq[t % 2], 16)

        @block.tensor
        def _(te):
            te.wait_ge(vs, c_vs)
            for t in range(NT):
                j, m = t // 8, t % 8
                if t >= 1:
                    te.wait_ge(vs, c_vs + (t - 1) * 6 + 1)
                for fh in range(2):
                    for c8 in range(8):
                        _mi = te.matmul(ps_m[:, fh, :],
                                        Obf[:, 8 * j + c8, m * 128:(m + 1) * 128],
                                        wT[:, c8, fh * 512:(fh + 1) * 512],
                                        start=(c8 == 0), stop=(c8 == 7))
                _mi.then_inc(tsm, 1)
                te.wait_ge(vs, c_vs + t * 6 + 2)
                if t >= 1:
                    te.wait_ge(s2, c_s2 + (t - 1) * 4 + 3)
                for dc in range(4):
                    _mi = te.transpose(ps_pt[:, dc * 128:(dc + 1) * 128],
                                       x1bf[:, t, dc * 128:(dc + 1) * 128], idn_s[:, :])
                _mi.then_inc(tsm, 1)
                te.wait_ge(s2, c_s2 + t * 4 + 2)
                for dc in range(4, 8):
                    _mi = te.transpose(ps_pt[:, (dc - 4) * 128:(dc - 3) * 128],
                                       x1bf[:, t, dc * 128:(dc + 1) * 128], idn_s[:, :])
                _mi.then_inc(tsm, 1)
                te.wait_ge(s2, c_s2 + t * 4 + 3)
                for dc in range(4):
                    _mi = te.transpose(ps_pt[:, dc * 128:(dc + 1) * 128],
                                       x1lo[:, dc * 128:(dc + 1) * 128], idn_s[:, :])
                _mi.then_inc(tsm, 1)
                te.wait_ge(vs, c_vs + t * 6 + 3)
                for dc in range(4, 8):
                    _mi = te.transpose(ps_pt[:, (dc - 4) * 128:(dc - 3) * 128],
                                       x1lo[:, dc * 128:(dc + 1) * 128], idn_s[:, :])
                _mi.then_inc(tsm, 1)
                te.wait_ge(vs, c_vs + t * 6 + 4)
                if t >= 1:
                    te.wait_ge(vs, c_vs + (t - 1) * 6 + 5)
                for pp in range(3):
                    for dc in range(8):
                        lh = xdTC if pp != 1 else xdTL
                        wh = wrs if pp != 2 else wrl
                        _mi = te.matmul(ps_t[:, 0:8], lh[:, dc, :], wh[:, dc, :],
                                        start=(pp == 0 and dc == 0),
                                        stop=(pp == 2 and dc == 7))
                _mi.then_inc(tsm, 1)
        C["ts"] += NT * 6

        @block.scalar
        def _(sc):
            for t in range(NT):
                sc.wait_ge(vs, c_vs + t * 6 + 1)
                sc.activation(cjunk[:, :], xstg[:, :], ACT.Square, bias=0.0, scale=1.0,
                              accum_out=stat[:, 1:2])
                sc.drain()
                sc.activation(stat[:, 1:2], stat[:, 1:2], ACT.Sqrt, bias=epss[:, 0:1],
                              scale=1.0 / D).then_inc(s2, 1)
                sc.wait_ge(tsm, c_ts + t * 6 + 2)
                for dc in range(4):
                    _si = sc.copy(xdTC[:, dc, :], ps_pt[:, dc * 128:(dc + 1) * 128])
                _si.then_inc(s2, 1)
                sc.wait_ge(tsm, c_ts + t * 6 + 3)
                for dc in range(4, 8):
                    _si = sc.copy(xdTC[:, dc, :],
                                  ps_pt[:, (dc - 4) * 128:(dc - 3) * 128])
                _si.then_inc(s2, 1)
                sc.wait_ge(vs, c_vs + t * 6 + 5)
                sc.activation(probs[:, t, :], lgts[:, :], ACT.Exp, bias=stat[:, 2:3],
                              scale=1.0, accum_out=stat[:, 3:4]).then_inc(s2, 1)
        C["s2"] += NT * 4

        @block.vector
        def _(ve):
            ve.wait_ge(sdc, NCONST)
            ve.wait_ge(dg, 16)
            ve.tensor_copy(cjunk[:, 0:64],
                           wrs.rearrange("p a e -> p (a e)")[:, :])
            ve.drain()
            ve.tensor_tensor(wrl.rearrange("p a e -> p (a e)")[:, :],
                             wrf.rearrange("p a e -> p (a e)")[:, :],
                             cjunk[:, 0:64], OP.subtract)
            ve.drain()
            for t in range(NT):
                ve.wait_ge(tsm, c_ts + t * 6 + 1)
                ve.wait_ge(sq[t % 2], (t // 2 + 1) * 16)
                if t >= 1:
                    ve.wait_ge(dgx, t * 16)
                if t >= 2:
                    ve.wait_ge(s2, c_s2 + (t - 2) * 4 + 4)
                for fh in range(2):
                    ve.tensor_add(xstg[:, fh * 512:(fh + 1) * 512],
                                  qres_b[:, t % 2, fh * 512:(fh + 1) * 512],
                                  ps_m[:, fh, :])
                ve.drain()
                ve.tensor_reduce(stat[:, 0:1], xstg[:, :], AXX, OP.add)
                ve.drain()
                ve.tensor_scalar(stat[:, 0:1], stat[:, 0:1], 1.0 / D, None, OP.mult)
                ve.drain()
                ve.tensor_scalar(xstg[:, :], xstg[:, :], stat[:, 0:1], None, OP.subtract)
                ve.drain()
                ve.engine_nop().then_inc(vs, 1)
                ve.wait_ge(s2, c_s2 + t * 4 + 1)
                ve.reciprocal(stat[:, 1:2], stat[:, 1:2])
                ve.drain()
                ve.tensor_scalar(x1bf[:, t, :], xstg[:, :], stat[:, 1:2], None, OP.mult)
                ve.drain()
                ve.tensor_scalar(xstg[:, :], xstg[:, :], stat[:, 1:2], None, OP.mult)
                ve.drain()
                ve.tensor_copy(x1sc[:, :], x1bf[:, t, :])
                ve.drain()
                ve.tensor_tensor(x1lo[:, :], xstg[:, :], x1sc[:, :], OP.subtract)
                ve.drain()
                ve.engine_nop().then_inc(vs, 1)
                ve.wait_ge(tsm, c_ts + t * 6 + 4)
                for dc in range(4):
                    ve.tensor_copy(xdTL[:, dc, :], ps_pt[:, dc * 128:(dc + 1) * 128])
                ve.drain()
                ve.engine_nop().then_inc(vs, 1)
                ve.wait_ge(tsm, c_ts + t * 6 + 5)
                for dc in range(4, 8):
                    ve.tensor_copy(xdTL[:, dc, :],
                                   ps_pt[:, (dc - 4) * 128:(dc - 3) * 128])
                ve.drain()
                ve.engine_nop().then_inc(vs, 1)
                ve.wait_ge(tsm, c_ts + t * 6 + 6)
                ve.tensor_add(lgts[:, :], ps_t[:, 0:8], brs[:, :])
                ve.drain()
                ve.tensor_reduce(stat[:, 2:3], lgts[:, :], AXX, OP.max, negate=True)
                ve.drain()
                ve.engine_nop().then_inc(vs, 1)
                ve.wait_ge(s2, c_s2 + t * 4 + 4)
                ve.reciprocal(stat[:, 3:4], stat[:, 3:4])
                ve.drain()
                ve.tensor_scalar(probs[:, t, :], probs[:, t, :], stat[:, 3:4], None,
                                 OP.mult)
                ve.drain()
                ve.engine_nop().then_inc(vs, 1)
        C["vs"] += NT * 6

        @block.gpsimd
        def _(g):
            for t in range(NT):
                g.wait_ge(vs, c_vs + t * 6 + 2)
                g.dma_start(out=x1_spill[t, :, :], in_=xstg[:, :]).then_inc(dgx, 16)

        d_vs, d_s2, d_ts, d_dg, d_cc, d_sd = (C["vs"], C["s2"], C["ts"], C["dg"],
                                              C["cc"], C["sd"])
        pr3 = probs
        i83 = iota8_s
        i8m3 = iota8m_s

        def bc(v):
            return v.rearrange("p (t o) -> p t o", o=1).broadcast_to([128, NT, 8])

        @block.vector
        def _(ve):
            ve.wait_ge(vs, d_vs)
            ve.tensor_reduce(g1v[:, :], pr3[:, :, :], AXX, OP.max)
            ve.drain()
            ve.tensor_tensor(scr8[:, :, :], pr3[:, :, :], bc(g1v), OP.is_equal)
            ve.drain()
            ve.tensor_tensor(scr8b[:, :, :], i8m3[:, :, :], scr8[:, :, :], OP.mult)
            ve.drain()
            ve.tensor_scalar(scr8b[:, :, :], scr8b[:, :, :], 8.0, None, OP.add)
            ve.drain()
            ve.tensor_reduce(e1v[:, :], scr8b[:, :, :], AXX, OP.min)
            ve.drain()
            ve.tensor_tensor(scr8[:, :, :], i83[:, :, :], bc(e1v), OP.is_equal)
            ve.drain()
            ve.scalar_tensor_tensor(scr8[:, :, :], scr8[:, :, :], -1.0e30,
                                    pr3[:, :, :], OP.mult, OP.add)
            ve.drain()
            ve.tensor_reduce(g2v[:, :], scr8[:, :, :], AXX, OP.max)
            ve.drain()
            ve.tensor_tensor(scr8b[:, :, :], scr8[:, :, :], bc(g2v), OP.is_equal)
            ve.drain()
            ve.tensor_tensor(scr8b[:, :, :], i8m3[:, :, :], scr8b[:, :, :], OP.mult)
            ve.drain()
            ve.tensor_scalar(scr8b[:, :, :], scr8b[:, :, :], 8.0, None, OP.add)
            ve.drain()
            ve.tensor_reduce(e2v[:, :], scr8b[:, :, :], AXX, OP.min)
            ve.drain()
            ve.engine_nop().then_inc(vs, 1)
        C["vs"] += 1

        @block.gpsimd
        def _(g):
            g.wait_ge(vs, C["vs"])
            g.dma_start(out=gates_loc[:, 0:16], in_=g1v[:, :]).then_inc(dg, 16)
            g.dma_start(out=gates_loc[:, 16:32], in_=g2v[:, :]).then_inc(dg, 16)
            g.dma_start(out=gates_loc[:, 32:48], in_=e1v[:, :]).then_inc(dg, 16)
            g.dma_start(out=gates_loc[:, 48:64], in_=e2v[:, :]).then_inc(dg, 16)
            g.wait_ge(dg, C["dg"] + 64)
            g.collective_compute("AllGather", OP.bypass,
                                 replica_groups=[list(range(NCORE))],
                                 ins=[gates_loc[:, :]], outs=[gates_all[:, :, :]]
                                 ).then_inc(cc, 1)
        C["dg"] += 64
        C["cc"] += 1

        @block.sync
        def _(sy):
            sy.wait_ge(cc, C["cc"])
            sy.dma_start(out=gall[:, :, :],
                         in_=gates_all.rearrange("i p f -> p i f")[:, :, :]
                         ).then_inc(sd, 16)
        C["sd"] += 16
        sd_gall = C["sd"]

        gfl = gall[:, :, 0:32]
        ifl = gall[:, :, 32:64]

        bis_vs, bis_ts = C["vs"], C["ts"]

        @block.vector
        def _(ve):
            ve.wait_ge(sd, sd_gall)
            for e in range(NE):
                ve.tensor_scalar(mete[:, e, :, :], ifl, float(e), None, OP.is_equal)
            ve.drain()
            ve.memset(thl[:, :], 0.0)
            ve.memset(thh[:, :], 1.0)
            ve.memset(thm[:, :], 0.5)
            ve.drain()
            ve.engine_nop().then_inc(vs, 1)
            for i in range(NITER):
                for e in range(NE):
                    ve.scalar_tensor_tensor(junkc.rearrange("p (i f) -> p i f", i=8),
                                            gfl, thm[:, e:e + 1],
                                            mete[:, e, :, :], OP.is_gt, OP.mult,
                                            accum_out=cntp[:, e:e + 1])
                    ve.drain()
                ve.engine_nop().then_inc(vs, 1)
                ve.wait_ge(tsm, bis_ts + i + 1)
                ve.tensor_scalar(cm1[:, :], ps_t[:, 0:8], float(CAP), None, OP.is_gt)
                ve.drain()
                ve.tensor_scalar(cm2[:, :], cm1[:, :], -1.0, 1.0, OP.mult, OP.add)
                ve.drain()
                ve.tensor_sub(ctt[:, :], thm[:, :], thl[:, :])
                ve.drain()
                ve.tensor_mul(ctt[:, :], ctt[:, :], cm1[:, :])
                ve.drain()
                ve.tensor_add(thl[:, :], thl[:, :], ctt[:, :])
                ve.drain()
                ve.tensor_sub(ctt[:, :], thm[:, :], thh[:, :])
                ve.drain()
                ve.tensor_mul(ctt[:, :], ctt[:, :], cm2[:, :])
                ve.drain()
                ve.tensor_add(thh[:, :], thh[:, :], ctt[:, :])
                ve.drain()
                ve.tensor_add(thm[:, :], thl[:, :], thh[:, :])
                ve.drain()
                ve.tensor_scalar(thm[:, :], thm[:, :], 0.5, None, OP.mult)
                ve.drain()
            ve.engine_nop().then_inc(vs, 1)
        C["vs"] += 2 + NITER

        @block.tensor
        def _(te):
            for i in range(NITER):
                te.wait_ge(vs, bis_vs + 1 + i + 1)
                te.matmul(ps_t[:, 0:8], ones_s[:, :], cntp[:, 0:8],
                          start=True, stop=True).then_inc(tsm, 1)
        C["ts"] += NITER

        r_vs, r_ts = C["vs"], C["ts"]

        @block.vector
        def _(ve):
            ve.memset(rnk[:, 0, :], 0.0)
            ve.memset(rnk[:, 1, :], 0.0)
            ve.drain()
            for j in range(2):
                ev = e1v if j == 0 else e2v
                gv = g1v if j == 0 else g2v
                ve.memset(thsel[:, :], 0.0)
                ve.drain()
                for e in range(NE):
                    ve.tensor_scalar(asc[:, :], ev[:, :], float(e), None, OP.is_equal)
                    ve.drain()
                    ve.scalar_tensor_tensor(thsel[:, :], asc[:, :], thh[:, e:e + 1],
                                            thsel[:, :], OP.mult, OP.add)
                    ve.drain()
                ve.tensor_tensor(vld[:, j, :], gv[:, :], thsel[:, :], OP.is_gt)
                ve.drain()
                for e in range(NE):
                    ve.tensor_scalar(asc[:, :], ev[:, :], float(e), None, OP.is_equal)
                    ve.drain()
                    ve.tensor_mul(asc[:, :], asc[:, :], vld[:, j, :])
                    ve.drain()
                    ve.tensor_reduce(rs8[:, j, e:e + 1], asc[:, :], AXX, OP.add)
                    ve.drain()
            ve.engine_nop().then_inc(vs, 1)
        C["vs"] += 1

        rk_vs, rk_ts = C["vs"], C["ts"]

        @block.tensor
        def _(te):
            te.wait_ge(vs, rk_vs)
            te.matmul(ps_t[:, 0:8], ltri_s[:, :], rs8[:, 0, 0:8], start=True,
                      stop=True).then_inc(tsm, 1)
            te.matmul(ps_m[:, 0, 0:8], ltri_s[:, :], rs8[:, 1, 0:8], start=True,
                      stop=True).then_inc(tsm, 1)
            te.matmul(ps_m[:, 1, 0:8], ones_s[:, :], rs8[:, 0, 0:8], start=True,
                      stop=True).then_inc(tsm, 1)
        C["ts"] += 3

        @block.vector
        def _(ve):
            ve.wait_ge(tsm, rk_ts + 3)
            ve.tensor_copy(off8[:, 0, :], ps_t[:, 0:8])
            ve.tensor_copy(off8[:, 1, :], ps_m[:, 0, 0:8])
            ve.tensor_copy(tot8[:, :], ps_m[:, 1, 0:8])
            ve.drain()
            ve.tensor_scalar(zrow_d[:, :], pio_s[:, :], float(NSLOT), None, OP.add)
            ve.tensor_scalar(zrow_g[:, :], pio_s[:, :], float(NCORE * NSLOT), None,
                             OP.add)
            ve.drain()
            ve.tensor_tensor_scan(tio16[:, :], onesr_s[:, :], onesr_s[:, :], 0.0,
                                  OP.add, OP.mult)
            ve.drain()
            for j in range(2):
                ev = e1v if j == 0 else e2v
                for e in range(NE):
                    ve.tensor_scalar(asc[:, :], ev[:, :], float(e), None, OP.is_equal)
                    ve.drain()
                    ve.tensor_mul(asc[:, :], asc[:, :], vld[:, j, :])
                    ve.drain()
                    ve.tensor_tensor_scan(asc2[:, :], asc[:, :], onesr_s[:, :], 0.0, OP.add, OP.mult)
                    ve.drain()
                    ve.tensor_sub(asc2[:, :], asc2[:, :], asc[:, :])
                    ve.drain()
                    ve.tensor_scalar(asc2[:, :], asc2[:, :], off8[:, j, e:e + 1], None,
                                     OP.add)
                    ve.drain()
                    if j == 1:
                        ve.tensor_scalar(asc2[:, :], asc2[:, :], tot8[:, e:e + 1], None,
                                         OP.add)
                        ve.drain()
                    ve.tensor_mul(asc2[:, :], asc2[:, :], asc[:, :])
                    ve.drain()
                    ve.tensor_add(rnk[:, j, :], rnk[:, j, :], asc2[:, :])
                    ve.drain()
            for j in range(2):
                ev = e1v if j == 0 else e2v
                ve.tensor_scalar(asc[:, :], rnk[:, j, :], float(PAD), None, OP.is_lt)
                ve.drain()
                ve.tensor_mul(asc[:, :], asc[:, :], vld[:, j, :])
                ve.drain()
                ve.tensor_scalar(zt16[:, :], tio16[:, :], float(j * NT - 1), None,
                                 OP.add)
                ve.drain()
                ve.tensor_scalar(zt16[:, :], zt16[:, :], 128.0, None, OP.mult)
                ve.drain()
                ve.tensor_scalar(zt16[:, :], zt16[:, :], zrow_d[:, 0:1], None, OP.add)
                ve.drain()
                ve.tensor_scalar(didxf[:, j, :], ev[:, :], float(PAD), None, OP.mult)
                ve.drain()
                ve.tensor_add(didxf[:, j, :], didxf[:, j, :], rnk[:, j, :])
                ve.drain()
                ve.tensor_sub(asc2[:, :], didxf[:, j, :], zt16[:, :])
                ve.drain()
                ve.tensor_mul(asc2[:, :], asc2[:, :], asc[:, :])
                ve.drain()
                ve.tensor_add(asc2[:, :], asc2[:, :], zt16[:, :])
                ve.drain()
                ve.tensor_copy(didx[:, j, :], asc2[:, :])
                ve.drain()
                ve.tensor_scalar(asc3[:, :], ev[:, :], float(NSLOT), None, OP.mult)
                ve.drain()
                ve.tensor_add(asc3[:, :], asc3[:, :], rnk[:, j, :])
                ve.drain()
                ve.tensor_scalar(asc3[:, :], asc3[:, :], cv[:, 1:2], None, OP.add)
                ve.drain()
                ve.tensor_scalar(asc2[:, :], asc3[:, :], zrow_g[:, 0:1], None,
                                 OP.subtract)
                ve.drain()
                ve.tensor_mul(asc2[:, :], asc2[:, :], asc[:, :])
                ve.drain()
                ve.tensor_scalar(asc2[:, :], asc2[:, :], zrow_g[:, 0:1], None, OP.add)
                ve.drain()
                ve.tensor_copy(gidx[:, j, :], asc2[:, :])
                ve.drain()
            ve.tensor_add(fidxf_s[:, :], fii_s[:, :], fir_s[:, :])
            ve.drain()
            ve.tensor_scalar(fidxf_s[:, :], fidxf_s[:, :], cv[:, 1:2], None, OP.add)
            ve.drain()
            ve.tensor_copy(fidx_s[:, :], fidxf_s[:, :])
            ve.drain()
            ve.engine_nop().then_inc(vs, 1)
        C["vs"] += 1

        @block.gpsimd
        def _(g):
            g.wait_ge(vs, C["vs"])
            for j in range(2):
                for t in range(NT):
                    g.indirect_dma_start(
                        out=disp_loc[:, :],
                        out_offset=IndirectOffsetOnAxis(ap=didx[:, j, t:t + 1], axis=0),
                        in_=x1bf[:, t, :], in_offset=None,
                        bounds_check=NSLOT + 32 * 128 - 1,
                        oob_is_err=False).then_inc(dg, 16)
                    _rfree()
            g.wait_ge(dg, C["dg"] + 32 * 16)
            g.collective_compute("AllGather", OP.bypass,
                                 replica_groups=[list(range(NCORE))],
                                 ins=[disp_loc[0:NSLOT, :]], outs=[disp_all[:, :]]
                                 ).then_inc(cc, 1)
        C["dg"] += 32 * 16
        C["cc"] += 1
        cc_disp = C["cc"]

        @block.gpsimd
        def _(g):
            g.wait_ge(vs, C["vs"])
            for hc in range(8):
                g.dma_start(out=w1s[:, :, hc * 512:(hc + 1) * 512],
                            in_=w1i[:, hc * 512:(hc + 1) * 512].rearrange(
                                "(dc p) h -> p dc h", p=128)[:, :, :]).then_inc(dg, 16)
            for fc in range(8):
                g.dma_start(out=w2s[:, fc * 4:(fc + 1) * 4, :],
                            in_=w2i[fc * 512:(fc + 1) * 512, :].rearrange(
                                "(hc p) f -> p hc f", p=128)[:, :, :]).then_inc(dg, 16)
        C["dg"] += 16 * 16

        @block.sync
        def _(sy):
            sy.wait_ge(cc, cc_disp)
            sy.dma_start(out=b2s[:, :], in_=b2i[:, :]).then_inc(sd, 16)
        C["sd"] += 16
        f_sd = C["sd"]

        f_dg = C["dg"]
        f_ts, f_s2, f_vs = C["ts"], C["s2"], C["vs"]

        @block.gpsimd
        def _(g):
            g.wait_ge(cc, cc_disp)
            for sgi in range(NSG):
                for st2 in range(2):
                    st = sgi * 2 + st2
                    if sgi >= 2 and st2 == 0:
                        g.wait_ge(tsm, f_ts + (sgi - 1) * 52)
                    g.indirect_dma_start(
                        out=xdisp[:, sgi % 2, st2, :], out_offset=None,
                        in_=disp_all[:, :],
                        in_offset=IndirectOffsetOnAxis(ap=fidx_s[:, st:st + 1], axis=0),
                        bounds_check=NCORE * NSLOT - 1, oob_is_err=False
                        ).then_inc(dgx2[sgi % 2], 16)
                    _rfree()
                if sgi >= 1:
                    for ss in range(2):
                        t = (sgi - 1) * 2 + ss
                        g.wait_ge(vs, f_vs + (sgi - 1) * 18 + 16 + ss + 1)
                        r0 = (sgi - 1) * SG + ss * 128
                        g.dma_start(out=ret_loc[r0:r0 + 128, :], in_=orowb[:, :]
                                    ).then_inc(dro, 16)

        @block.tensor
        def _(te):
            te.wait_ge(dg, f_dg)
            for sgi in range(NSG):
                te.wait_ge(dgx2[sgi % 2], (sgi // 2 + 1) * 32)
                for st2 in range(2):
                    for dc in range(8):
                        i = st2 * 8 + dc
                        if i >= 2:
                            te.wait_ge(vs, f_vs + sgi * 18 + i - 1)
                        te.transpose(ps_pt[:, (i % 2) * 128:(i % 2) * 128 + 128],
                                     xdisp[:, sgi % 2, st2, dc * 128:(dc + 1) * 128],
                                     idn_s[:, :]).then_inc(tsm, 1)
                te.wait_ge(vs, f_vs + sgi * 18 + 16)
                for hc in range(32):
                    if hc >= 2:
                        te.wait_ge(s2, f_s2 + sgi * 36 + hc - 1)
                    for dc in range(8):
                        _mi = te.matmul(ps_m[:, hc % 2, 0:SG],
                                        w1s[:, dc, hc * 128:(hc + 1) * 128],
                                        xdT[:, dc, :], start=(dc == 0), stop=(dc == 7))
                    _mi.then_inc(tsm, 1)
                te.wait_ge(s2, f_s2 + sgi * 36 + 32)
                for ss in range(2):
                    if ss == 1:
                        te.wait_ge(s2, f_s2 + sgi * 36 + 34)
                    for fh in range(2):
                        for hc in range(32):
                            _mi = te.matmul(ps_pv[:, fh, :],
                                            hsb[:, hc, ss * 128:(ss + 1) * 128],
                                            w2s[:, hc, fh * 512:(fh + 1) * 512],
                                            start=(hc == 0), stop=(hc == 31))
                        _mi.then_inc(tsm, 1)
        C["ts"] += NSG * 52

        @block.scalar
        def _(sc):
            for sgi in range(NSG):
                for hc in range(32):
                    sc.wait_ge(tsm, f_ts + sgi * 52 + 16 + hc + 1)
                    if hc == 0 and sgi >= 1:
                        sc.wait_ge(tsm, f_ts + sgi * 52)
                    sc.activation(hsb[:, hc, :], ps_m[:, hc % 2, 0:SG], ACT.Silu,
                                  bias=b1s[:, hc:hc + 1], scale=1.0).then_inc(s2, 1)
                for ss in range(2):
                    t = sgi * 2 + ss
                    if t >= 1:
                        if ss == 1:
                            sc.wait_ge(vs, f_vs + sgi * 18 + 17)
                        else:
                            sc.wait_ge(vs, f_vs + sgi * 18)
                    for fh in range(2):
                        sc.wait_ge(tsm, f_ts + sgi * 52 + 48 + 2 * ss + fh + 1)
                        sc.copy(orow[:, fh * 512:(fh + 1) * 512], ps_pv[:, fh, :]
                                ).then_inc(s2, 1)
        C["s2"] += NSG * 36

        @block.vector
        def _(ve):
            ve.wait_ge(sd, f_sd)
            for sgi in range(NSG):
                for st2 in range(2):
                    for dc in range(8):
                        i = st2 * 8 + dc
                        ve.wait_ge(tsm, f_ts + sgi * 52 + i + 1)
                        ve.tensor_copy(xdT[:, dc, st2 * 128:(st2 + 1) * 128],
                                       ps_pt[:, (i % 2) * 128:(i % 2) * 128 + 128])
                        ve.drain()
                        ve.engine_nop().then_inc(vs, 1)
                for ss in range(2):
                    t = sgi * 2 + ss
                    ve.wait_ge(s2, f_s2 + sgi * 36 + 32 + 2 * ss + 2)
                    if t >= 1:
                        ve.wait_ge(dro, t * 16)
                    ve.tensor_add(orowb[:, :], orow[:, :], b2s[:, :])
                    ve.drain()
                    ve.engine_nop().then_inc(vs, 1)
        C["vs"] += NSG * 18

        @block.gpsimd
        def _(g):
            for ss in range(2):
                t = (NSG - 1) * 2 + ss
                g.wait_ge(vs, f_vs + (NSG - 1) * 18 + 16 + ss + 1)
                r0 = (NSG - 1) * SG + ss * 128
                g.dma_start(out=ret_loc[r0:r0 + 128, :], in_=orowb[:, :]
                            ).then_inc(dro, 16)
            g.wait_ge(dro, NSG * 2 * 16)
            g.collective_compute("AllGather", OP.bypass,
                                 replica_groups=[list(range(NCORE))],
                                 ins=[ret_loc[:, :]], outs=[ret_all[0:NCORE * NSLOT, :]]
                                 ).then_inc(cc, 1)
        C["cc"] += 1
        cc_ret = C["cc"]

        g_vs, g_s2, g_sd = C["vs"], C["s2"], C["sd"]

        @block.sync
        def _(sy):
            sy.wait_ge(cc, cc_ret)
            sy.wait_ge(dgx, NT * 16)
            for t in range(NT):
                if t >= 1:
                    sy.wait_ge(vs, g_vs + t * 2)
                sy.dma_start(out=x1rl[:, :] if t % 2 == 0 else qres_b[:, 0, :],
                             in_=x1_spill[t, :, :]).then_inc(sd, 16)
        C["sd"] += NT * 16

        @block.scalar
        def _(sc):
            for t in range(NT):
                sc.wait_ge(vs, g_vs + t * 2 + 1)
                sc.activation(orow[:, :], yrow[:, :], ACT.Square, bias=0.0, scale=1.0,
                              accum_out=stat[:, 1:2])
                sc.drain()
                sc.activation(stat[:, 1:2], stat[:, 1:2], ACT.Sqrt, bias=epss[:, 0:1],
                              scale=1.0 / D).then_inc(s2, 1)
        C["s2"] += NT

        @block.vector
        def _(ve):
            for t in range(NT):
                ve.wait_ge(dgg, (t + 1) * 32)
                ve.wait_ge(sd, g_sd + (t + 1) * 16)
                if t >= 1:
                    ve.wait_ge(dgo, t * 16)
                xv = x1rl[:, :] if t % 2 == 0 else qres_b[:, 0, :]
                ve.tensor_scalar(yrow[:, :], ga0[:, :], g1v[:, t:t + 1], None, OP.mult)
                ve.drain()
                ve.scalar_tensor_tensor(yrow[:, :], ga1[:, :], g2v[:, t:t + 1],
                                        yrow[:, :], OP.mult, OP.add)
                ve.drain()
                ve.tensor_add(yrow[:, :], yrow[:, :], xv)
                ve.drain()
                ve.tensor_reduce(stat[:, 0:1], yrow[:, :], AXX, OP.add)
                ve.drain()
                ve.tensor_scalar(stat[:, 0:1], stat[:, 0:1], 1.0 / D, None, OP.mult)
                ve.drain()
                ve.tensor_scalar(yrow[:, :], yrow[:, :], stat[:, 0:1], None,
                                 OP.subtract)
                ve.drain()
                ve.engine_nop().then_inc(vs, 1)
                ve.wait_ge(s2, g_s2 + t + 1)
                ve.reciprocal(stat[:, 1:2], stat[:, 1:2])
                ve.drain()
                ve.tensor_scalar(yrow[:, :], yrow[:, :], stat[:, 1:2], None, OP.mult)
                ve.drain()
                ve.engine_nop().then_inc(vs, 1)
        C["vs"] += NT * 2

        @block.gpsimd
        def _(g):
            g.wait_ge(cc, cc_ret)
            for t in range(NT):
                j, m = t // 8, t % 8
                if t >= 1:
                    g.wait_ge(vs, g_vs + t * 2)
                g.indirect_dma_start(
                    out=ga0[:, :], out_offset=None, in_=ret_all[:, :],
                    in_offset=IndirectOffsetOnAxis(ap=gidx[:, 0, t:t + 1], axis=0),
                    bounds_check=NCORE * NSLOT + 127, oob_is_err=False
                    ).then_inc(dgg, 16)
                _rfree()
                g.indirect_dma_start(
                    out=ga1[:, :], out_offset=None, in_=ret_all[:, :],
                    in_offset=IndirectOffsetOnAxis(ap=gidx[:, 1, t:t + 1], axis=0),
                    bounds_check=NCORE * NSLOT + 127, oob_is_err=False
                    ).then_inc(dgg, 16)
                _rfree()
                g.wait_ge(vs, g_vs + t * 2 + 2)
                g.dma_start(out=out[j, m * 128:(m + 1) * 128, :], in_=yrow[:, :]
                            ).then_inc(dgo, 16)
            g.wait_ge(dgo, NT * 16)

    return nc
    return nc


_NC_CACHE = [None]


def kernel(L0_attn_q, L0_attn_wq, L0_attn_wk, L0_attn_wv, L0_attn_wo,
           L0_ffn_Wr, L0_ffn_br, L0_ffn_W1, L0_ffn_b1, L0_ffn_W2, L0_ffn_b2):
    q = np.ascontiguousarray(np.asarray(L0_attn_q, dtype=np.float32).reshape(S, D))
    q3 = q.reshape(KI, 16, D)
    wqh = np.ascontiguousarray(np.asarray(L0_attn_wq, np.float32))
    wkh = np.ascontiguousarray(np.asarray(L0_attn_wk, np.float32))
    wvh = np.ascontiguousarray(np.asarray(L0_attn_wv, np.float32))
    woh = np.ascontiguousarray(np.asarray(L0_attn_wo, np.float32))
    wrh = np.ascontiguousarray(np.asarray(L0_ffn_Wr, np.float32))
    brh = np.asarray(L0_ffn_br, np.float32)
    w1h = np.asarray(L0_ffn_W1, np.float32)
    b1h = np.asarray(L0_ffn_b1, np.float32)
    w2h = np.asarray(L0_ffn_W2, np.float32)
    b2h = np.asarray(L0_ffn_b2, np.float32)

    if _NC_CACHE[0] is None:
        _NC_CACHE[0] = build_nc()
    nc = _NC_CACHE[0]

    brr = np.ascontiguousarray(np.broadcast_to(brh, (128, NE)))
    in_maps = []
    for c in range(NCORE):
        b1c = np.ascontiguousarray(b1h[c].reshape(HID // 128, 128).T)
        in_maps.append({
            "q_own": q[SOWN * c:SOWN * (c + 1)],
            "q_res": np.ascontiguousarray(q3[:, 2 * c:2 * c + 2, :].transpose(1, 0, 2)),
            "wq": wqh, "wk": wkh, "wv": wvh, "wo": woh,
            "wr": wrh, "brr": brr,
            "w1i": np.ascontiguousarray(w1h[c]),
            "b1i": b1c,
            "w2i": np.ascontiguousarray(w2h[c]),
            "b2i": np.ascontiguousarray(np.broadcast_to(b2h[c], (128, D))),
            "cvec": np.ascontiguousarray(
                np.broadcast_to(np.array([c, PAD * c, NSLOT * c, 0], np.float32),
                                (128, 4))),
        })
    res = run_bass_kernel_spmd(nc, in_maps, core_ids=list(range(NCORE)))
    outp = np.empty((KI, 16, D), np.float32)
    for c in range(NCORE):
        outp[:, 2 * c:2 * c + 2, :] = np.asarray(res.results[c]["out"]).reshape(2, KI, D).transpose(1, 0, 2)
    return outp.reshape(1, S, D)


# version 07
# speedup vs baseline: 1.0209x; 1.0209x over previous
import numpy as np
import ml_dtypes
import concourse.bass as bass
import concourse.mybir as mybir
from concourse.bass_core_rust import IndirectOffsetOnAxis
from concourse.bass_utils import run_bass_kernel_spmd

F32 = mybir.dt.float32
BF16 = mybir.dt.bfloat16
U32 = mybir.dt.uint32
OP = mybir.AluOpType
ACT = mybir.ActivationFunctionType
AXX = mybir.AxisListType.X

S = 16384
D = 1024
HID = 4096
NE = 8
NCORE = 8
SOWN = 2048
NT = 16
CAP = 2458
PAD = 448
NSLOT = NE * PAD
KB = 512
NKB = S // KB
NITER = 24
EPS = 1e-5
SG = 256
NSG = NSLOT // SG
KI = 1024
BIG = 1.0e6


def build_nc():
    nc = bass.Bass("TRN2", target_bir_lowering=False)
    ap = lambda t: t.ap()

    q_own = ap(nc.dram_tensor("q_own", [SOWN, D], F32, kind="ExternalInput"))
    q_res = ap(nc.dram_tensor("q_res", [2, KI, D], F32, kind="ExternalInput"))
    wq = ap(nc.dram_tensor("wq", [D, D], F32, kind="ExternalInput"))
    wk = ap(nc.dram_tensor("wk", [D, D], F32, kind="ExternalInput"))
    wv = ap(nc.dram_tensor("wv", [D, D], F32, kind="ExternalInput"))
    wo = ap(nc.dram_tensor("wo", [D, D], F32, kind="ExternalInput"))
    wr = ap(nc.dram_tensor("wr", [D, NE], F32, kind="ExternalInput"))
    brr = ap(nc.dram_tensor("brr", [128, NE], F32, kind="ExternalInput"))
    w1i = ap(nc.dram_tensor("w1i", [D, HID], F32, kind="ExternalInput"))
    b1i = ap(nc.dram_tensor("b1i", [128, HID // 128], F32, kind="ExternalInput"))
    w2i = ap(nc.dram_tensor("w2i", [HID, D], F32, kind="ExternalInput"))
    b2i = ap(nc.dram_tensor("b2i", [128, D], F32, kind="ExternalInput"))
    cvec = ap(nc.dram_tensor("cvec", [128, 4], F32, kind="ExternalInput"))
    out = ap(nc.dram_tensor("out", [2, KI, D], F32, kind="ExternalOutput"))

    idnb = ap(nc.inline_tensor(np.eye(128, dtype=np.float32).astype(ml_dtypes.bfloat16),
                               "c_idnb"))
    ones_c = ap(nc.inline_tensor(np.ones((128, 128), np.float32), "c_ones"))
    ltri_c = ap(nc.inline_tensor(
        (np.arange(128)[:, None] < np.arange(128)[None, :]).astype(np.float32), "c_ltri"))
    io8 = np.broadcast_to(np.arange(8, dtype=np.float32), (128, NT, 8)).reshape(128, 128)
    iota8_c = ap(nc.inline_tensor(io8.copy(), "c_iota8"))
    iota8m_c = ap(nc.inline_tensor((io8 - 8.0).copy(), "c_iota8m"))
    onesr_c = ap(nc.inline_tensor(np.ones((128, NT), np.float32), "c_onesr"))
    sglob = np.arange(NSLOT).reshape(NSLOT // 128, 128)
    fidx_i = ap(nc.inline_tensor((sglob // PAD * NSLOT).T.astype(np.float32).copy(),
                                 "c_fidxi"))
    fidx_r = ap(nc.inline_tensor((sglob % PAD).T.astype(np.float32).copy(), "c_fidxr"))
    pio_c = ap(nc.inline_tensor(np.arange(128, dtype=np.float32).reshape(128, 1).copy(),
                                "c_pio"))

    kv_k_loc = ap(nc.dram_tensor("kv_k_loc", [D, SOWN], BF16))
    kv_v_loc = ap(nc.dram_tensor("kv_v_loc", [SOWN, D], BF16))
    k_all = ap(nc.dram_tensor("k_all", [NCORE, D, SOWN], BF16))
    v_all = ap(nc.dram_tensor("v_all", [S, D], BF16))
    gates_loc = ap(nc.dram_tensor("gates_loc", [128, 64], F32))
    gates_all = ap(nc.dram_tensor("gates_all", [NCORE, 128, 64], F32))
    disp_loc = ap(nc.dram_tensor("disp_loc", [NSLOT + 32 * 128, D], BF16))
    disp_all = ap(nc.dram_tensor("disp_all", [NCORE * NSLOT, D], BF16))
    ret_loc = ap(nc.dram_tensor("ret_loc", [NSLOT, D], BF16))
    ret_all = ap(nc.dram_tensor("ret_all", [NCORE * NSLOT + 128, D], BF16))
    x1_spill = ap(nc.dram_tensor("x1_spill", [NT, 128, D], F32))

    K = 1024
    SB0 = 16512
    AT = lambda name, shape, dt, o: nc.alloc_sbuf_tensor_at(name, shape, dt,
                                                            offset=SB0 + o)
    qT = AT("qT", [128, 8, SOWN], BF16, 0)
    Ksb = AT("Ksb", [128, 8, SOWN], BF16, 32 * K)
    Osb = AT("Osb", [128, NT, D], F32, 0)
    w2s = AT("w2s", [128, 32, D], BF16, 0)
    Qf = AT("Qf", [128, 8, SOWN], BF16, 64 * K)
    zdisp = AT("zdisp", [128, NSLOT // 128, D], BF16, 64 * K)
    Vsb = AT("Vsb", [128, NT, D], BF16, 96 * K)
    Obf = AT("Obf", [128, NT, D], BF16, 96 * K)
    w1s = AT("w1s", [128, 8, HID], BF16, 64 * K)
    x1bf = AT("x1bf", [128, NT, D], BF16, 128 * K)
    hsb = AT("hsb", [128, 32, SG], BF16, 128 * K)
    xdT = AT("xdT", [128, 8, SG], BF16, 144 * K)
    xdisp = AT("xdisp", [128, 2, SG // 128, D], BF16, 152 * K)
    xdTC = AT("xdTC", [128, 8, 128], BF16, 0)
    cjunk = AT("cjunk", [128, D], F32, 4 * K)
    x1sc = AT("x1sc", [128, D], F32, 8 * K)
    x1lo = AT("x1lo", [128, D], BF16, 12 * K)
    xdTL = AT("xdTL", [128, 8, 128], BF16, 14 * K)
    wT = AT("wT", [128, 8, D], BF16, 160 * K)
    kbuf = AT("kbuf", [128, 2, 8, KB], BF16, 160 * K)
    orow = AT("orow", [128, D], F32, 160 * K)
    orowb = AT("orowb", [128, D], BF16, 164 * K)
    ga0 = AT("ga0", [128, 2, D], BF16, 166 * K)
    ga1 = AT("ga1", [128, 2, D], BF16, 8 * K)
    yrow = AT("yrow", [128, D], F32, 170 * K)
    x1rl = AT("x1rl", [128, D], F32, 174 * K)
    b2s = AT("b2s", [128, D], F32, 178 * K)
    vbuf = AT("vbuf", [128, 2, 4, D], BF16, 176 * K)
    wstage = AT("wstage", [128, 2, D], F32, 176 * K)
    wstgb = AT("wstgb", [128, 2, D], BF16, 184 * K)
    qres_b = AT("qres_b", [128, 2, D], F32, 176 * K)
    xstg = AT("xstg", [128, D], F32, 184 * K)
    off = [192 * K]

    def perm(name, shape, dt):
        t = AT(name, shape, dt, off[0])
        n = int(np.prod(shape[1:])) * mybir.dt.size(dt)
        off[0] += (n + 31) & ~31
        return t

    idn_s = perm("idn_s", [128, 128], BF16)
    ones_s = perm("ones_s", [128, 128], F32)
    ltri_s = perm("ltri_s", [128, 128], F32)
    iota8_s = perm("iota8_s", [128, NT, 8], F32)
    iota8m_s = perm("iota8m_s", [128, NT, 8], F32)
    onesr_s = perm("onesr_s", [128, NT], F32)
    fii_s = perm("fii_s", [128, NSLOT // 128], F32)
    fir_s = perm("fir_s", [128, NSLOT // 128], F32)
    fidxf_s = perm("fidxf_s", [128, NSLOT // 128], F32)
    fidx_s = perm("fidx_s", [128, NSLOT // 128], U32)
    cv = perm("cv", [128, 4], F32)
    wrs = perm("wrs", [128, 8, NE], BF16)
    brs = perm("brs", [128, NE], F32)
    b1s = perm("b1s", [128, HID // 128], F32)
    mst = perm("mst", [128, NT], F32)
    lsum = perm("lsum", [128, NT], F32)
    nmn = perm("nmn", [128, 2], F32)
    dmm = perm("dmm", [128, 2], F32)
    rmx = perm("rmx", [128, 1], F32)
    alp = perm("alp", [128, 2], F32)
    lbb = perm("lbb", [128, 2], F32)
    psb = perm("psb", [128, KB], BF16)
    ptb = perm("ptb", [128, KB], BF16)
    oinv = perm("oinv", [128, 1], F32)
    probs = perm("probs", [128, NT, NE], F32)
    scr8 = perm("scr8", [128, NT, NE], F32)
    scr8b = perm("scr8b", [128, NT, NE], F32)
    g1v = perm("g1v", [128, NT], F32)
    g2v = perm("g2v", [128, NT], F32)
    e1v = perm("e1v", [128, NT], F32)
    e2v = perm("e2v", [128, NT], F32)
    gall = perm("gall", [128, 8, 64], F32)
    mete = perm("mete", [128, 8, 8, 32], BF16)
    junkc = perm("junkc", [128, 256], BF16)
    thl = perm("thl", [128, NE], F32)
    thh = perm("thh", [128, NE], F32)
    thm = perm("thm", [128, NE], F32)
    cntp = perm("cntp", [128, NE], F32)
    cm1 = perm("cm1", [128, NE], F32)
    cm2 = perm("cm2", [128, NE], F32)
    ctt = perm("ctt", [128, NE], F32)
    thsel = perm("thsel", [128, NT], F32)
    vld = perm("vld", [128, 2, NT], F32)
    rnk = perm("rnk", [128, 2, NT], F32)
    rs8 = perm("rs8", [128, 2, NE], F32)
    off8 = perm("off8", [128, 2, NE], F32)
    tot8 = perm("tot8", [128, NE], F32)
    asc = perm("asc", [128, NT], F32)
    asc2 = perm("asc2", [128, NT], F32)
    asc3 = perm("asc3", [128, NT], F32)
    didxf = perm("didxf", [128, 2, NT], F32)
    didx = perm("didx", [128, 2, NT], U32)
    gidx = perm("gidx", [128, 2, NT], U32)
    stat = perm("stat", [128, 4], F32)
    lgts = perm("lgts", [128, NE], F32)
    epss = perm("epss", [128, 1], F32)
    wrf = perm("wrf", [128, NE, NE], F32)
    wrl = perm("wrl", [128, NE, NE], BF16)
    pio_s = perm("pio_s", [128, 1], F32)
    zrow_d = perm("zrow_d", [128, 1], F32)
    zrow_g = perm("zrow_g", [128, 1], F32)
    tio16 = perm("tio16", [128, NT], F32)
    zt16 = perm("zt16", [128, NT], F32)
    assert SB0 + off[0] <= 229344, (SB0, off[0])

    ps_s = nc.alloc_psum_tensor("ps_s", [128, KB], F32)
    ps_s2 = nc.alloc_psum_tensor("ps_s2", [128, KB], F32)
    ps_pt = nc.alloc_psum_tensor("ps_pt", [128, KB], BF16)
    ps_pv = nc.alloc_psum_tensor("ps_pv", [128, 2, KB], F32)
    ps_m = nc.alloc_psum_tensor("ps_m", [128, 2, KB], F32)
    ps_t = nc.alloc_psum_tensor("ps_t", [128, 128], F32)

    sd = nc.alloc_semaphore("sd")
    sdc = nc.alloc_semaphore("sdc")
    sdl0 = nc.alloc_semaphore("sdl0")
    sdl1 = nc.alloc_semaphore("sdl1")
    sdk0 = nc.alloc_semaphore("sdk0")
    sdk1 = nc.alloc_semaphore("sdk1")
    sq0 = nc.alloc_semaphore("sq0")
    sq1 = nc.alloc_semaphore("sq1")
    dg = nc.alloc_semaphore("dg")
    dgx = nc.alloc_semaphore("dgx")
    dgz = nc.alloc_semaphore("dgz")
    dgx2a = nc.alloc_semaphore("dgx2a")
    dgx2b = nc.alloc_semaphore("dgx2b")
    dro = nc.alloc_semaphore("dro")
    dgg = nc.alloc_semaphore("dgg")
    dgg2 = nc.alloc_semaphore("dgg2")
    dgo = nc.alloc_semaphore("dgo")
    cc = nc.alloc_semaphore("cc")
    vs = nc.alloc_semaphore("vs")
    s2 = nc.alloc_semaphore("s2")
    tsm = nc.alloc_semaphore("ts")
    with nc.Block() as block:
        sdl = [sdl0, sdl1]
        sdk = [sdk0, sdk1]
        sq = [sq0, sq1]
        dgx2 = [dgx2a, dgx2b]
        C = {"sd": 0, "dg": 0, "cc": 0, "vs": 0, "s2": 0, "ts": 0}
        POOL = mybir.EngineType.Pool
        _rind = [nc.alloc_register(POOL, "rind%d" % i) for i in range(3)]
        for _r in _rind:
            nc.free_register(_r)

        def _rfree():
            for _r in _rind:
                nc.free_register(_r)

        @block.sync
        def _(sy):
            sy.dma_start(out=idn_s[:, :], in_=idnb[:, :]).then_inc(sdc, 16)
            sy.dma_start(out=ones_s[:, :], in_=ones_c[:, :]).then_inc(sdc, 16)
            sy.dma_start(out=ltri_s[:, :], in_=ltri_c[:, :]).then_inc(sdc, 16)
            sy.dma_start(out=iota8_s.rearrange("p a e -> p (a e)")[:, :],
                         in_=iota8_c[:, :]).then_inc(sdc, 16)
            sy.dma_start(out=iota8m_s.rearrange("p a e -> p (a e)")[:, :],
                         in_=iota8m_c[:, :]).then_inc(sdc, 16)
            sy.dma_start(out=onesr_s[:, :], in_=onesr_c[:, :]).then_inc(sdc, 16)
            sy.dma_start(out=fii_s[:, :], in_=fidx_i[:, :]).then_inc(sdc, 16)
            sy.dma_start(out=fir_s[:, :], in_=fidx_r[:, :]).then_inc(sdc, 16)
            sy.dma_start(out=cv[:, :], in_=cvec[:, :]).then_inc(sdc, 16)
            sy.dma_start(out=brs[:, :], in_=brr[:, :]).then_inc(sdc, 16)
            sy.dma_start(out=b1s[:, :], in_=b1i[:, :]).then_inc(sdc, 16)
            sy.dma_start(out=wrf[:, :, :],
                         in_=wr.rearrange("(dc p) e -> p dc e", p=128)[:, :, :]
                         ).then_inc(sdc, 16)
            sy.dma_start(out=pio_s[:, :], in_=pio_c[:, :]).then_inc(sdc, 16)
        NCONST = 13 * 16


        LT = {"n": [0, 0]}

        def load_T(src, ntile, dest_fn, gate=None):
            b_s2, b_ts, b_vs = C["s2"], C["ts"], C["vs"]
            b_lt = list(LT["n"])

            @block.sync
            def _(sy):
                if gate is not None:
                    sy.wait_ge(gate[0], gate[1])
                sy.wait_ge(dg, 16)
                sy.wait_ge(s2, b_s2)
                for t in range(ntile):
                    if t >= 2:
                        sy.wait_ge(s2, b_s2 + t - 1)
                    sy.dma_start(out=wstage[:, t % 2, :],
                                 in_=src[t * 128:(t + 1) * 128, :]
                                 ).then_inc(sdl[t % 2], 16)

            @block.scalar
            def _(sc):
                sc.wait_ge(tsm, b_ts)
                for t in range(ntile):
                    sc.wait_ge(sdl[t % 2], (b_lt[t % 2] + t // 2 + 1) * 16)
                    if t >= 2:
                        sc.wait_ge(tsm, b_ts + (t - 1) * 8)
                    sc.activation(wstgb[:, t % 2, :], wstage[:, t % 2, :], ACT.Copy,
                                  bias=0.0, scale=1.0).then_inc(s2, 1)
            LT["n"][0] += (ntile + 1) // 2
            LT["n"][1] += ntile // 2

            @block.tensor
            def _(te):
                te.wait_ge(sdc, NCONST)
                te.wait_ge(vs, b_vs)
                for t in range(ntile):
                    te.wait_ge(s2, b_s2 + t + 1)
                    for dc in range(8):
                        i = t * 8 + dc
                        if i >= 2:
                            te.wait_ge(vs, b_vs + i - 1)
                        te.transpose(ps_pt[:, (i % 2) * 128:(i % 2) * 128 + 128],
                                     wstgb[:, t % 2, dc * 128:(dc + 1) * 128],
                                     idn_s[:, :]).then_inc(tsm, 1)

            @block.vector
            def _(ve):
                for t in range(ntile):
                    for dc in range(8):
                        i = t * 8 + dc
                        ve.wait_ge(tsm, b_ts + i + 1)
                        ve.tensor_copy(dest_fn(t, dc),
                                       ps_pt[:, (i % 2) * 128:(i % 2) * 128 + 128])
                        ve.drain()
                        ve.engine_nop().then_inc(vs, 1)

            C["s2"] += ntile
            C["ts"] += ntile * 8
            C["vs"] += ntile * 8

        @block.vector
        def _(ve):
            ve.memset(zdisp[:, :, :], 0.0)
            ve.drain()
            ve.engine_nop().then_inc(vs, 1)
        C["vs"] += 1

        @block.gpsimd
        def _(g):
            g.wait_ge(vs, C["vs"])
            for zc in range(4):
                g.dma_start(out=disp_loc.rearrange("(a p) d -> p a d", p=128)[:, zc * 7:(zc + 1) * 7, :],
                            in_=zdisp[:, zc * 7:(zc + 1) * 7, :]).then_inc(dgz, 16)
            g.dma_start(out=ret_all[NCORE * NSLOT:NCORE * NSLOT + 128, :],
                        in_=zdisp[:, 0, :]).then_inc(dgz, 16)
            g.dma_start(out=wrs[:, :, :],
                        in_=wr.rearrange("(dc p) e -> p dc e", p=128)[:, :, :]
                        ).then_inc(dg, 16)
        DGZ = 80
        C["dg"] += 16

        load_T(q_own, NT, lambda t, dc: qT[:, dc, t * 128:(t + 1) * 128],
               gate=(dgz, DGZ))

        def proj(wsrc, kind):
            load_T(wsrc, 8, lambda t, dc: wT[:, dc, t * 128:(t + 1) * 128])
            b_ts, b_s2, b_vs = C["ts"], C["s2"], C["vs"]
            if kind in ("K", "Q"):
                dest = Ksb if kind == "K" else Qf
                scl = 0.125 if kind == "Q" else 1.0

                @block.tensor
                def _(te):
                    te.wait_ge(vs, b_vs)
                    n = 0
                    for ec in range(8):
                        for sh in range(4):
                            n += 1
                            if n > 2:
                                te.wait_ge(s2, b_s2 + n - 2)
                            for dc in range(8):
                                _mi = te.matmul(ps_m[:, n % 2, :],
                                                wT[:, dc, ec * 128:(ec + 1) * 128],
                                                qT[:, dc, sh * 512:(sh + 1) * 512],
                                                start=(dc == 0), stop=(dc == 7))
                            _mi.then_inc(tsm, 1)

                @block.scalar
                def _(sc):
                    if kind == "Q":
                        sc.nop()
                    n = 0
                    for ec in range(8):
                        for sh in range(4):
                            n += 1
                            sc.wait_ge(tsm, b_ts + n)
                            sc.activation(dest[:, ec, sh * 512:(sh + 1) * 512],
                                          ps_m[:, n % 2, :], ACT.Copy, bias=0.0,
                                          scale=scl).then_inc(s2, 1)
            else:
                @block.tensor
                def _(te):
                    te.wait_ge(vs, b_vs)
                    n = 0
                    for st in range(NT):
                        for eh in range(2):
                            n += 1
                            if n > 2:
                                te.wait_ge(s2, b_s2 + n - 2)
                            for dc in range(8):
                                _mi = te.matmul(ps_m[:, n % 2, :],
                                                qT[:, dc, st * 128:(st + 1) * 128],
                                                wT[:, dc, eh * 512:(eh + 1) * 512],
                                                start=(dc == 0), stop=(dc == 7))
                            _mi.then_inc(tsm, 1)

                @block.scalar
                def _(sc):
                    sc.nop()
                    n = 0
                    for st in range(NT):
                        for eh in range(2):
                            n += 1
                            sc.wait_ge(tsm, b_ts + n)
                            sc.activation(Vsb[:, st, eh * 512:(eh + 1) * 512],
                                          ps_m[:, n % 2, :], ACT.Copy, bias=0.0,
                                          scale=1.0).then_inc(s2, 1)
            C["ts"] += 32
            C["s2"] += 32

        proj(wk, "K")
        proj(wv, "V")
        proj(wq, "Q")

        dg_a3 = C["dg"]

        @block.gpsimd
        def _(g):
            g.wait_ge(s2, C["s2"])
            for zc in range(2):
                g.dma_start(out=kv_k_loc.rearrange("(c p) s -> p c s", p=128)[:, zc * 4:(zc + 1) * 4, :],
                            in_=Ksb[:, zc * 4:(zc + 1) * 4, :]).then_inc(dg, 16)
                g.dma_start(out=kv_v_loc.rearrange("(t p) e -> p t e", p=128)[:, zc * 8:(zc + 1) * 8, :],
                            in_=Vsb[:, zc * 8:(zc + 1) * 8, :]).then_inc(dg, 16)
            g.wait_ge(dg, dg_a3 + 64)
            g.collective_compute("AllGather", OP.bypass,
                                 replica_groups=[list(range(NCORE))],
                                 ins=[kv_k_loc[:, :]], outs=[k_all[:, :, :]]
                                 ).then_inc(cc, 1)
            g.collective_compute("AllGather", OP.bypass,
                                 replica_groups=[list(range(NCORE))],
                                 ins=[kv_v_loc[:, :]], outs=[v_all[:, :]]
                                 ).then_inc(cc, 1)
        C["dg"] += 64
        C["cc"] += 2

        @block.vector
        def _(ve):
            ve.wait_ge(tsm, C["ts"])
            ve.wait_ge(dg, dg_a3 + 64)
            ve.memset(mst[:, :], -3.0e38)
            ve.memset(lsum[:, :], 0.0)
            ve.memset(epss[:, :], EPS)
            ve.memset(Osb[:, :, :], 0.0)
            ve.drain()
            ve.engine_nop().then_inc(vs, 1)
        C["vs"] += 1

        b_ts, b_vs, b_s2 = C["ts"], C["vs"], C["s2"]
        NIT = NKB * NT

        @block.sync
        def _(sy):
            sy.wait_ge(cc, C["cc"])
            for kb in range(NKB):
                if kb >= 2:
                    sy.wait_ge(tsm, b_ts + 3 * NT * (kb - 1) + 1)
                r, s0 = kb // 4, (kb % 4) * KB
                sy.dma_start(out=kbuf[:, kb % 2, :, :],
                             in_=k_all[r, :, s0:s0 + KB].rearrange(
                                 "(c p) s -> p c s", p=128)[:, :, :]
                             ).then_inc(sdk[kb % 2], 16)
                sy.dma_start(out=vbuf[:, kb % 2, :, :],
                             in_=v_all[kb * KB:(kb + 1) * KB, :].rearrange(
                                 "(c p) e -> p c e", p=128)[:, :, :]
                             ).then_inc(sdk[kb % 2], 16)

        def s_cum(j):
            return 1 if j == 0 else 3 * j - 1

        def t_cum(it):
            return 3 * it + 3 if it < NIT - 1 else 3 * it + 2

        def pv_cum(it):
            return 3 * it + 4 if it < NIT - 1 else 3 * it + 3

        @block.tensor
        def _(te):
            def sgrp(j):
                kbj, qtj = j // NT, j % NT
                sN = ps_s if j % 2 == 0 else ps_s2
                if qtj == 0:
                    te.wait_ge(sdk[kbj % 2], (kbj // 2 + 1) * 32)
                if j >= 2:
                    te.wait_ge(s2, b_s2 + 3 * (j - 2) + 1)
                for ec in range(8):
                    _mi = te.matmul(sN[:, :], Qf[:, ec, qtj * 128:(qtj + 1) * 128],
                                    kbuf[:, kbj % 2, ec, :],
                                    start=(ec == 0), stop=(ec == 7))
                _mi.then_inc(tsm, 1)
            sgrp(0)
            for kb in range(NKB):
                for qt in range(NT):
                    it = kb * NT + qt
                    if it + 1 < NIT:
                        sgrp(it + 1)
                    te.wait_ge(s2, b_s2 + 3 * it + 1)
                    for kc in range(4):
                        _mi = te.transpose(ps_pt[:, kc * 128:(kc + 1) * 128],
                                           psb[:, kc * 128:(kc + 1) * 128], idn_s[:, :])
                    _mi.then_inc(tsm, 1)
                    te.wait_ge(s2, b_s2 + 3 * it + 3)
                    if it >= 1:
                        te.wait_ge(vs, b_vs + 2 * it)
                    for kc in range(4):
                        for eh in range(2):
                            _mi = te.matmul(ps_pv[:, eh, :],
                                            ptb[:, kc * 128:(kc + 1) * 128],
                                            vbuf[:, kb % 2, kc, eh * 512:(eh + 1) * 512],
                                            start=(kc == 0), stop=(kc == 3))
                    _mi.then_inc(tsm, 1)
        C["ts"] += NIT * 3

        @block.vector
        def _(ve):
            for kb in range(NKB):
                for qt in range(NT):
                    it = kb * NT + qt
                    sN = ps_s if it % 2 == 0 else ps_s2
                    ve.wait_ge(tsm, b_ts + s_cum(it))
                    if it >= 2:
                        ve.wait_ge(s2, b_s2 + 3 * (it - 2) + 2)
                    ve.tensor_reduce(rmx[:, :], sN[:, :], AXX, OP.max)
                    ve.drain()
                    ve.tensor_max(rmx[:, :], rmx[:, :], mst[:, qt:qt + 1])
                    ve.drain()
                    ve.tensor_scalar(nmn[:, it % 2:it % 2 + 1], rmx[:, :], -1.0, None,
                                     OP.mult)
                    ve.tensor_sub(dmm[:, it % 2:it % 2 + 1], mst[:, qt:qt + 1],
                                  rmx[:, :])
                    ve.drain()
                    ve.tensor_copy(mst[:, qt:qt + 1], rmx[:, :])
                    ve.drain()
                    ve.engine_nop().then_inc(vs, 1)
                    ve.wait_ge(tsm, b_ts + pv_cum(it))
                    ve.wait_ge(s2, b_s2 + 3 * it + 2)
                    for eh in range(2):
                        ve.scalar_tensor_tensor(
                            Osb[:, qt, eh * 512:(eh + 1) * 512],
                            Osb[:, qt, eh * 512:(eh + 1) * 512],
                            alp[:, it % 2:it % 2 + 1], ps_pv[:, eh, :],
                            OP.mult, OP.add)
                    ve.drain()
                    ve.scalar_tensor_tensor(lsum[:, qt:qt + 1], lsum[:, qt:qt + 1],
                                            alp[:, it % 2:it % 2 + 1],
                                            lbb[:, it % 2:it % 2 + 1], OP.mult, OP.add)
                    ve.drain()
                    ve.engine_nop().then_inc(vs, 1)
        C["vs"] += NIT * 2

        @block.scalar
        def _(sc):
            for kb in range(NKB):
                for qt in range(NT):
                    it = kb * NT + qt
                    sN = ps_s if it % 2 == 0 else ps_s2
                    sc.wait_ge(vs, b_vs + 2 * it + 1)
                    if it >= 2:
                        sc.wait_ge(vs, b_vs + 2 * (it - 2) + 2)
                    if it >= 1:
                        sc.wait_ge(tsm, b_ts + t_cum(it - 1))
                    sc.activation(psb[:, :], sN[:, :], ACT.Exp,
                                  bias=nmn[:, it % 2:it % 2 + 1], scale=1.0,
                                  accum_out=lbb[:, it % 2:it % 2 + 1]).then_inc(s2, 1)
                    sc.activation(alp[:, it % 2:it % 2 + 1], dmm[:, it % 2:it % 2 + 1],
                                  ACT.Exp, bias=0.0, scale=1.0).then_inc(s2, 1)
                    sc.wait_ge(tsm, b_ts + t_cum(it))
                    sc.activation(ptb[:, :], ps_pt[:, :], ACT.Copy, bias=0.0,
                                  scale=1.0).then_inc(s2, 1)
        C["s2"] += NIT * 3

        @block.vector
        def _(ve):
            for t in range(NT):
                ve.reciprocal(oinv[:, :], lsum[:, t:t + 1])
                ve.drain()
                ve.tensor_scalar(Obf[:, t, :], Osb[:, t, :], oinv[:, 0:1], None, OP.mult)
                ve.drain()
            ve.engine_nop().then_inc(vs, 1)
        C["vs"] += 1
        vs_obf = C["vs"]

        load_T(wo, 8, lambda t, dc: wT[:, dc, t * 128:(t + 1) * 128],
               gate=(vs, vs_obf))
        c_ts, c_vs, c_s2 = C["ts"], C["vs"], C["s2"]

        @block.sync
        def _(sy):
            sy.wait_ge(s2, c_s2)
            for t in range(NT):
                j, m = t // 8, t % 8
                if t >= 2:
                    sy.wait_ge(vs, c_vs + (t - 2) * 6 + 1)
                sy.dma_start(out=qres_b[:, t % 2, :],
                             in_=q_res[j, m * 128:(m + 1) * 128, :]
                             ).then_inc(sq[t % 2], 16)

        @block.tensor
        def _(te):
            te.wait_ge(vs, c_vs)
            for t in range(NT):
                j, m = t // 8, t % 8
                if t >= 1:
                    te.wait_ge(vs, c_vs + (t - 1) * 6 + 1)
                for fh in range(2):
                    for c8 in range(8):
                        _mi = te.matmul(ps_m[:, fh, :],
                                        Obf[:, 8 * j + c8, m * 128:(m + 1) * 128],
                                        wT[:, c8, fh * 512:(fh + 1) * 512],
                                        start=(c8 == 0), stop=(c8 == 7))
                _mi.then_inc(tsm, 1)
                te.wait_ge(vs, c_vs + t * 6 + 2)
                if t >= 1:
                    te.wait_ge(s2, c_s2 + (t - 1) * 4 + 3)
                for dc in range(4):
                    _mi = te.transpose(ps_pt[:, dc * 128:(dc + 1) * 128],
                                       x1bf[:, t, dc * 128:(dc + 1) * 128], idn_s[:, :])
                _mi.then_inc(tsm, 1)
                te.wait_ge(s2, c_s2 + t * 4 + 2)
                for dc in range(4, 8):
                    _mi = te.transpose(ps_pt[:, (dc - 4) * 128:(dc - 3) * 128],
                                       x1bf[:, t, dc * 128:(dc + 1) * 128], idn_s[:, :])
                _mi.then_inc(tsm, 1)
                te.wait_ge(s2, c_s2 + t * 4 + 3)
                for dc in range(4):
                    _mi = te.transpose(ps_pt[:, dc * 128:(dc + 1) * 128],
                                       x1lo[:, dc * 128:(dc + 1) * 128], idn_s[:, :])
                _mi.then_inc(tsm, 1)
                te.wait_ge(vs, c_vs + t * 6 + 3)
                for dc in range(4, 8):
                    _mi = te.transpose(ps_pt[:, (dc - 4) * 128:(dc - 3) * 128],
                                       x1lo[:, dc * 128:(dc + 1) * 128], idn_s[:, :])
                _mi.then_inc(tsm, 1)
                te.wait_ge(vs, c_vs + t * 6 + 4)
                if t >= 1:
                    te.wait_ge(vs, c_vs + (t - 1) * 6 + 5)
                for pp in range(3):
                    for dc in range(8):
                        lh = xdTC if pp != 1 else xdTL
                        wh = wrs if pp != 2 else wrl
                        _mi = te.matmul(ps_t[:, 0:8], lh[:, dc, :], wh[:, dc, :],
                                        start=(pp == 0 and dc == 0),
                                        stop=(pp == 2 and dc == 7))
                _mi.then_inc(tsm, 1)
        C["ts"] += NT * 6

        @block.scalar
        def _(sc):
            for t in range(NT):
                sc.wait_ge(vs, c_vs + t * 6 + 1)
                sc.activation(cjunk[:, :], xstg[:, :], ACT.Square, bias=0.0, scale=1.0,
                              accum_out=stat[:, 1:2])
                sc.drain()
                sc.activation(stat[:, 1:2], stat[:, 1:2], ACT.Sqrt, bias=epss[:, 0:1],
                              scale=1.0 / D).then_inc(s2, 1)
                sc.wait_ge(tsm, c_ts + t * 6 + 2)
                for dc in range(4):
                    _si = sc.copy(xdTC[:, dc, :], ps_pt[:, dc * 128:(dc + 1) * 128])
                _si.then_inc(s2, 1)
                sc.wait_ge(tsm, c_ts + t * 6 + 3)
                for dc in range(4, 8):
                    _si = sc.copy(xdTC[:, dc, :],
                                  ps_pt[:, (dc - 4) * 128:(dc - 3) * 128])
                _si.then_inc(s2, 1)
                sc.wait_ge(vs, c_vs + t * 6 + 5)
                sc.activation(probs[:, t, :], lgts[:, :], ACT.Exp, bias=stat[:, 2:3],
                              scale=1.0, accum_out=stat[:, 3:4]).then_inc(s2, 1)
        C["s2"] += NT * 4

        @block.vector
        def _(ve):
            ve.wait_ge(sdc, NCONST)
            ve.wait_ge(dg, 16)
            ve.tensor_copy(cjunk[:, 0:64],
                           wrs.rearrange("p a e -> p (a e)")[:, :])
            ve.drain()
            ve.tensor_tensor(wrl.rearrange("p a e -> p (a e)")[:, :],
                             wrf.rearrange("p a e -> p (a e)")[:, :],
                             cjunk[:, 0:64], OP.subtract)
            ve.drain()
            for t in range(NT):
                ve.wait_ge(tsm, c_ts + t * 6 + 1)
                ve.wait_ge(sq[t % 2], (t // 2 + 1) * 16)
                if t >= 1:
                    ve.wait_ge(dgx, t * 16)
                if t >= 2:
                    ve.wait_ge(s2, c_s2 + (t - 2) * 4 + 4)
                for fh in range(2):
                    ve.tensor_add(xstg[:, fh * 512:(fh + 1) * 512],
                                  qres_b[:, t % 2, fh * 512:(fh + 1) * 512],
                                  ps_m[:, fh, :])
                ve.drain()
                ve.tensor_reduce(stat[:, 0:1], xstg[:, :], AXX, OP.add)
                ve.drain()
                ve.tensor_scalar(stat[:, 0:1], stat[:, 0:1], 1.0 / D, None, OP.mult)
                ve.drain()
                ve.tensor_scalar(xstg[:, :], xstg[:, :], stat[:, 0:1], None, OP.subtract)
                ve.drain()
                ve.engine_nop().then_inc(vs, 1)
                ve.wait_ge(s2, c_s2 + t * 4 + 1)
                ve.reciprocal(stat[:, 1:2], stat[:, 1:2])
                ve.drain()
                ve.tensor_scalar(x1bf[:, t, :], xstg[:, :], stat[:, 1:2], None, OP.mult)
                ve.drain()
                ve.tensor_scalar(xstg[:, :], xstg[:, :], stat[:, 1:2], None, OP.mult)
                ve.drain()
                ve.tensor_copy(x1sc[:, :], x1bf[:, t, :])
                ve.drain()
                ve.tensor_tensor(x1lo[:, :], xstg[:, :], x1sc[:, :], OP.subtract)
                ve.drain()
                ve.engine_nop().then_inc(vs, 1)
                ve.wait_ge(tsm, c_ts + t * 6 + 4)
                for dc in range(4):
                    ve.tensor_copy(xdTL[:, dc, :], ps_pt[:, dc * 128:(dc + 1) * 128])
                ve.drain()
                ve.engine_nop().then_inc(vs, 1)
                ve.wait_ge(tsm, c_ts + t * 6 + 5)
                for dc in range(4, 8):
                    ve.tensor_copy(xdTL[:, dc, :],
                                   ps_pt[:, (dc - 4) * 128:(dc - 3) * 128])
                ve.drain()
                ve.engine_nop().then_inc(vs, 1)
                ve.wait_ge(tsm, c_ts + t * 6 + 6)
                ve.tensor_add(lgts[:, :], ps_t[:, 0:8], brs[:, :])
                ve.drain()
                ve.tensor_reduce(stat[:, 2:3], lgts[:, :], AXX, OP.max, negate=True)
                ve.drain()
                ve.engine_nop().then_inc(vs, 1)
                ve.wait_ge(s2, c_s2 + t * 4 + 4)
                ve.reciprocal(stat[:, 3:4], stat[:, 3:4])
                ve.drain()
                ve.tensor_scalar(probs[:, t, :], probs[:, t, :], stat[:, 3:4], None,
                                 OP.mult)
                ve.drain()
                ve.engine_nop().then_inc(vs, 1)
        C["vs"] += NT * 6

        @block.gpsimd
        def _(g):
            for t in range(NT):
                g.wait_ge(vs, c_vs + t * 6 + 2)
                g.dma_start(out=x1_spill[t, :, :], in_=xstg[:, :]).then_inc(dgx, 16)

        d_vs, d_s2, d_ts, d_dg, d_cc, d_sd = (C["vs"], C["s2"], C["ts"], C["dg"],
                                              C["cc"], C["sd"])
        pr3 = probs
        i83 = iota8_s
        i8m3 = iota8m_s

        def bc(v):
            return v.rearrange("p (t o) -> p t o", o=1).broadcast_to([128, NT, 8])

        @block.vector
        def _(ve):
            ve.wait_ge(vs, d_vs)
            ve.tensor_reduce(g1v[:, :], pr3[:, :, :], AXX, OP.max)
            ve.drain()
            ve.tensor_tensor(scr8[:, :, :], pr3[:, :, :], bc(g1v), OP.is_equal)
            ve.drain()
            ve.tensor_tensor(scr8b[:, :, :], i8m3[:, :, :], scr8[:, :, :], OP.mult)
            ve.drain()
            ve.tensor_scalar(scr8b[:, :, :], scr8b[:, :, :], 8.0, None, OP.add)
            ve.drain()
            ve.tensor_reduce(e1v[:, :], scr8b[:, :, :], AXX, OP.min)
            ve.drain()
            ve.tensor_tensor(scr8[:, :, :], i83[:, :, :], bc(e1v), OP.is_equal)
            ve.drain()
            ve.scalar_tensor_tensor(scr8[:, :, :], scr8[:, :, :], -1.0e30,
                                    pr3[:, :, :], OP.mult, OP.add)
            ve.drain()
            ve.tensor_reduce(g2v[:, :], scr8[:, :, :], AXX, OP.max)
            ve.drain()
            ve.tensor_tensor(scr8b[:, :, :], scr8[:, :, :], bc(g2v), OP.is_equal)
            ve.drain()
            ve.tensor_tensor(scr8b[:, :, :], i8m3[:, :, :], scr8b[:, :, :], OP.mult)
            ve.drain()
            ve.tensor_scalar(scr8b[:, :, :], scr8b[:, :, :], 8.0, None, OP.add)
            ve.drain()
            ve.tensor_reduce(e2v[:, :], scr8b[:, :, :], AXX, OP.min)
            ve.drain()
            ve.engine_nop().then_inc(vs, 1)
        C["vs"] += 1

        @block.gpsimd
        def _(g):
            g.wait_ge(vs, C["vs"])
            g.dma_start(out=gates_loc[:, 0:16], in_=g1v[:, :]).then_inc(dg, 16)
            g.dma_start(out=gates_loc[:, 16:32], in_=g2v[:, :]).then_inc(dg, 16)
            g.dma_start(out=gates_loc[:, 32:48], in_=e1v[:, :]).then_inc(dg, 16)
            g.dma_start(out=gates_loc[:, 48:64], in_=e2v[:, :]).then_inc(dg, 16)
            g.wait_ge(dg, C["dg"] + 64)
            g.collective_compute("AllGather", OP.bypass,
                                 replica_groups=[list(range(NCORE))],
                                 ins=[gates_loc[:, :]], outs=[gates_all[:, :, :]]
                                 ).then_inc(cc, 1)
        C["dg"] += 64
        C["cc"] += 1

        @block.sync
        def _(sy):
            sy.wait_ge(cc, C["cc"])
            sy.dma_start(out=gall[:, :, :],
                         in_=gates_all.rearrange("i p f -> p i f")[:, :, :]
                         ).then_inc(sd, 16)
        C["sd"] += 16
        sd_gall = C["sd"]

        gfl = gall[:, :, 0:32]
        ifl = gall[:, :, 32:64]

        bis_vs, bis_ts = C["vs"], C["ts"]

        @block.vector
        def _(ve):
            ve.wait_ge(sd, sd_gall)
            for e in range(NE):
                ve.tensor_scalar(mete[:, e, :, :], ifl, float(e), None, OP.is_equal)
            ve.drain()
            ve.memset(thl[:, :], 0.0)
            ve.memset(thh[:, :], 1.0)
            ve.memset(thm[:, :], 0.5)
            ve.drain()
            ve.engine_nop().then_inc(vs, 1)
            for i in range(NITER):
                for e in range(NE):
                    ve.scalar_tensor_tensor(junkc.rearrange("p (i f) -> p i f", i=8),
                                            gfl, thm[:, e:e + 1],
                                            mete[:, e, :, :], OP.is_gt, OP.mult,
                                            accum_out=cntp[:, e:e + 1])
                    ve.drain()
                ve.engine_nop().then_inc(vs, 1)
                ve.wait_ge(tsm, bis_ts + i + 1)
                ve.tensor_scalar(cm1[:, :], ps_t[:, 0:8], float(CAP), None, OP.is_gt)
                ve.drain()
                ve.tensor_scalar(cm2[:, :], cm1[:, :], -1.0, 1.0, OP.mult, OP.add)
                ve.drain()
                ve.tensor_sub(ctt[:, :], thm[:, :], thl[:, :])
                ve.drain()
                ve.tensor_mul(ctt[:, :], ctt[:, :], cm1[:, :])
                ve.drain()
                ve.tensor_add(thl[:, :], thl[:, :], ctt[:, :])
                ve.drain()
                ve.tensor_sub(ctt[:, :], thm[:, :], thh[:, :])
                ve.drain()
                ve.tensor_mul(ctt[:, :], ctt[:, :], cm2[:, :])
                ve.drain()
                ve.tensor_add(thh[:, :], thh[:, :], ctt[:, :])
                ve.drain()
                ve.tensor_add(thm[:, :], thl[:, :], thh[:, :])
                ve.drain()
                ve.tensor_scalar(thm[:, :], thm[:, :], 0.5, None, OP.mult)
                ve.drain()
            ve.engine_nop().then_inc(vs, 1)
        C["vs"] += 2 + NITER

        @block.tensor
        def _(te):
            for i in range(NITER):
                te.wait_ge(vs, bis_vs + 1 + i + 1)
                te.matmul(ps_t[:, 0:8], ones_s[:, :], cntp[:, 0:8],
                          start=True, stop=True).then_inc(tsm, 1)
        C["ts"] += NITER

        r_vs, r_ts = C["vs"], C["ts"]

        @block.vector
        def _(ve):
            ve.memset(rnk[:, 0, :], 0.0)
            ve.memset(rnk[:, 1, :], 0.0)
            ve.drain()
            for j in range(2):
                ev = e1v if j == 0 else e2v
                gv = g1v if j == 0 else g2v
                ve.memset(thsel[:, :], 0.0)
                ve.drain()
                for e in range(NE):
                    ve.tensor_scalar(asc[:, :], ev[:, :], float(e), None, OP.is_equal)
                    ve.drain()
                    ve.scalar_tensor_tensor(thsel[:, :], asc[:, :], thh[:, e:e + 1],
                                            thsel[:, :], OP.mult, OP.add)
                    ve.drain()
                ve.tensor_tensor(vld[:, j, :], gv[:, :], thsel[:, :], OP.is_gt)
                ve.drain()
                for e in range(NE):
                    ve.tensor_scalar(asc[:, :], ev[:, :], float(e), None, OP.is_equal)
                    ve.drain()
                    ve.tensor_mul(asc[:, :], asc[:, :], vld[:, j, :])
                    ve.drain()
                    ve.tensor_reduce(rs8[:, j, e:e + 1], asc[:, :], AXX, OP.add)
                    ve.drain()
            ve.engine_nop().then_inc(vs, 1)
        C["vs"] += 1

        rk_vs, rk_ts = C["vs"], C["ts"]

        @block.tensor
        def _(te):
            te.wait_ge(vs, rk_vs)
            te.matmul(ps_t[:, 0:8], ltri_s[:, :], rs8[:, 0, 0:8], start=True,
                      stop=True).then_inc(tsm, 1)
            te.matmul(ps_m[:, 0, 0:8], ltri_s[:, :], rs8[:, 1, 0:8], start=True,
                      stop=True).then_inc(tsm, 1)
            te.matmul(ps_m[:, 1, 0:8], ones_s[:, :], rs8[:, 0, 0:8], start=True,
                      stop=True).then_inc(tsm, 1)
        C["ts"] += 3

        @block.vector
        def _(ve):
            ve.wait_ge(tsm, rk_ts + 3)
            ve.tensor_copy(off8[:, 0, :], ps_t[:, 0:8])
            ve.tensor_copy(off8[:, 1, :], ps_m[:, 0, 0:8])
            ve.tensor_copy(tot8[:, :], ps_m[:, 1, 0:8])
            ve.drain()
            ve.tensor_scalar(zrow_d[:, :], pio_s[:, :], float(NSLOT), None, OP.add)
            ve.tensor_scalar(zrow_g[:, :], pio_s[:, :], float(NCORE * NSLOT), None,
                             OP.add)
            ve.drain()
            ve.tensor_tensor_scan(tio16[:, :], onesr_s[:, :], onesr_s[:, :], 0.0,
                                  OP.add, OP.mult)
            ve.drain()
            for j in range(2):
                ev = e1v if j == 0 else e2v
                for e in range(NE):
                    ve.tensor_scalar(asc[:, :], ev[:, :], float(e), None, OP.is_equal)
                    ve.drain()
                    ve.tensor_mul(asc[:, :], asc[:, :], vld[:, j, :])
                    ve.drain()
                    ve.tensor_tensor_scan(asc2[:, :], asc[:, :], onesr_s[:, :], 0.0, OP.add, OP.mult)
                    ve.drain()
                    ve.tensor_sub(asc2[:, :], asc2[:, :], asc[:, :])
                    ve.drain()
                    ve.tensor_scalar(asc2[:, :], asc2[:, :], off8[:, j, e:e + 1], None,
                                     OP.add)
                    ve.drain()
                    if j == 1:
                        ve.tensor_scalar(asc2[:, :], asc2[:, :], tot8[:, e:e + 1], None,
                                         OP.add)
                        ve.drain()
                    ve.tensor_mul(asc2[:, :], asc2[:, :], asc[:, :])
                    ve.drain()
                    ve.tensor_add(rnk[:, j, :], rnk[:, j, :], asc2[:, :])
                    ve.drain()
            for j in range(2):
                ev = e1v if j == 0 else e2v
                ve.tensor_scalar(asc[:, :], rnk[:, j, :], float(PAD), None, OP.is_lt)
                ve.drain()
                ve.tensor_mul(asc[:, :], asc[:, :], vld[:, j, :])
                ve.drain()
                ve.tensor_scalar(zt16[:, :], tio16[:, :], float(j * NT - 1), None,
                                 OP.add)
                ve.drain()
                ve.tensor_scalar(zt16[:, :], zt16[:, :], 128.0, None, OP.mult)
                ve.drain()
                ve.tensor_scalar(zt16[:, :], zt16[:, :], zrow_d[:, 0:1], None, OP.add)
                ve.drain()
                ve.tensor_scalar(didxf[:, j, :], ev[:, :], float(PAD), None, OP.mult)
                ve.drain()
                ve.tensor_add(didxf[:, j, :], didxf[:, j, :], rnk[:, j, :])
                ve.drain()
                ve.tensor_sub(asc2[:, :], didxf[:, j, :], zt16[:, :])
                ve.drain()
                ve.tensor_mul(asc2[:, :], asc2[:, :], asc[:, :])
                ve.drain()
                ve.tensor_add(asc2[:, :], asc2[:, :], zt16[:, :])
                ve.drain()
                ve.tensor_copy(didx[:, j, :], asc2[:, :])
                ve.drain()
                ve.tensor_scalar(asc3[:, :], ev[:, :], float(NSLOT), None, OP.mult)
                ve.drain()
                ve.tensor_add(asc3[:, :], asc3[:, :], rnk[:, j, :])
                ve.drain()
                ve.tensor_scalar(asc3[:, :], asc3[:, :], cv[:, 1:2], None, OP.add)
                ve.drain()
                ve.tensor_scalar(asc2[:, :], asc3[:, :], zrow_g[:, 0:1], None,
                                 OP.subtract)
                ve.drain()
                ve.tensor_mul(asc2[:, :], asc2[:, :], asc[:, :])
                ve.drain()
                ve.tensor_scalar(asc2[:, :], asc2[:, :], zrow_g[:, 0:1], None, OP.add)
                ve.drain()
                ve.tensor_copy(gidx[:, j, :], asc2[:, :])
                ve.drain()
            ve.tensor_add(fidxf_s[:, :], fii_s[:, :], fir_s[:, :])
            ve.drain()
            ve.tensor_scalar(fidxf_s[:, :], fidxf_s[:, :], cv[:, 1:2], None, OP.add)
            ve.drain()
            ve.tensor_copy(fidx_s[:, :], fidxf_s[:, :])
            ve.drain()
            ve.engine_nop().then_inc(vs, 1)
        C["vs"] += 1

        @block.gpsimd
        def _(g):
            g.wait_ge(vs, C["vs"])
            for j in range(2):
                for t in range(NT):
                    g.indirect_dma_start(
                        out=disp_loc[:, :],
                        out_offset=IndirectOffsetOnAxis(ap=didx[:, j, t:t + 1], axis=0),
                        in_=x1bf[:, t, :], in_offset=None,
                        bounds_check=NSLOT + 32 * 128 - 1,
                        oob_is_err=False).then_inc(dg, 16)
                    _rfree()
            g.wait_ge(dg, C["dg"] + 32 * 16)
            g.collective_compute("AllGather", OP.bypass,
                                 replica_groups=[list(range(NCORE))],
                                 ins=[disp_loc[0:NSLOT, :]], outs=[disp_all[:, :]]
                                 ).then_inc(cc, 1)
        C["dg"] += 32 * 16
        C["cc"] += 1
        cc_disp = C["cc"]

        @block.gpsimd
        def _(g):
            g.wait_ge(vs, C["vs"])
            for hc in range(8):
                g.dma_start(out=w1s[:, :, hc * 512:(hc + 1) * 512],
                            in_=w1i[:, hc * 512:(hc + 1) * 512].rearrange(
                                "(dc p) h -> p dc h", p=128)[:, :, :]).then_inc(dg, 16)
            for fc in range(8):
                g.dma_start(out=w2s[:, fc * 4:(fc + 1) * 4, :],
                            in_=w2i[fc * 512:(fc + 1) * 512, :].rearrange(
                                "(hc p) f -> p hc f", p=128)[:, :, :]).then_inc(dg, 16)
        C["dg"] += 16 * 16

        @block.sync
        def _(sy):
            sy.wait_ge(cc, cc_disp)
            sy.dma_start(out=b2s[:, :], in_=b2i[:, :]).then_inc(sd, 16)
        C["sd"] += 16
        f_sd = C["sd"]

        f_dg = C["dg"]
        f_ts, f_s2, f_vs = C["ts"], C["s2"], C["vs"]

        @block.gpsimd
        def _(g):
            g.wait_ge(cc, cc_disp)
            for sgi in range(NSG):
                for st2 in range(2):
                    st = sgi * 2 + st2
                    if sgi >= 2 and st2 == 0:
                        g.wait_ge(tsm, f_ts + (sgi - 1) * 52)
                    g.indirect_dma_start(
                        out=xdisp[:, sgi % 2, st2, :], out_offset=None,
                        in_=disp_all[:, :],
                        in_offset=IndirectOffsetOnAxis(ap=fidx_s[:, st:st + 1], axis=0),
                        bounds_check=NCORE * NSLOT - 1, oob_is_err=False
                        ).then_inc(dgx2[sgi % 2], 16)
                    _rfree()
                if sgi >= 1:
                    for ss in range(2):
                        t = (sgi - 1) * 2 + ss
                        g.wait_ge(vs, f_vs + (sgi - 1) * 18 + 16 + ss + 1)
                        r0 = (sgi - 1) * SG + ss * 128
                        g.dma_start(out=ret_loc[r0:r0 + 128, :], in_=orowb[:, :]
                                    ).then_inc(dro, 16)

        @block.tensor
        def _(te):
            te.wait_ge(dg, f_dg)
            for sgi in range(NSG):
                te.wait_ge(dgx2[sgi % 2], (sgi // 2 + 1) * 32)
                for st2 in range(2):
                    for dc in range(8):
                        i = st2 * 8 + dc
                        if i >= 2:
                            te.wait_ge(vs, f_vs + sgi * 18 + i - 1)
                        te.transpose(ps_pt[:, (i % 2) * 128:(i % 2) * 128 + 128],
                                     xdisp[:, sgi % 2, st2, dc * 128:(dc + 1) * 128],
                                     idn_s[:, :]).then_inc(tsm, 1)
                te.wait_ge(vs, f_vs + sgi * 18 + 16)
                for hc in range(32):
                    if hc >= 2:
                        te.wait_ge(s2, f_s2 + sgi * 36 + hc - 1)
                    for dc in range(8):
                        _mi = te.matmul(ps_m[:, hc % 2, 0:SG],
                                        w1s[:, dc, hc * 128:(hc + 1) * 128],
                                        xdT[:, dc, :], start=(dc == 0), stop=(dc == 7))
                    _mi.then_inc(tsm, 1)
                te.wait_ge(s2, f_s2 + sgi * 36 + 32)
                for ss in range(2):
                    if ss == 1:
                        te.wait_ge(s2, f_s2 + sgi * 36 + 34)
                    for fh in range(2):
                        for hc in range(32):
                            _mi = te.matmul(ps_pv[:, fh, :],
                                            hsb[:, hc, ss * 128:(ss + 1) * 128],
                                            w2s[:, hc, fh * 512:(fh + 1) * 512],
                                            start=(hc == 0), stop=(hc == 31))
                        _mi.then_inc(tsm, 1)
        C["ts"] += NSG * 52

        @block.scalar
        def _(sc):
            for sgi in range(NSG):
                for hc in range(32):
                    sc.wait_ge(tsm, f_ts + sgi * 52 + 16 + hc + 1)
                    if hc == 0 and sgi >= 1:
                        sc.wait_ge(tsm, f_ts + sgi * 52)
                    sc.activation(hsb[:, hc, :], ps_m[:, hc % 2, 0:SG], ACT.Silu,
                                  bias=b1s[:, hc:hc + 1], scale=1.0).then_inc(s2, 1)
                for ss in range(2):
                    t = sgi * 2 + ss
                    if t >= 1:
                        if ss == 1:
                            sc.wait_ge(vs, f_vs + sgi * 18 + 17)
                        else:
                            sc.wait_ge(vs, f_vs + sgi * 18)
                    for fh in range(2):
                        sc.wait_ge(tsm, f_ts + sgi * 52 + 48 + 2 * ss + fh + 1)
                        sc.copy(orow[:, fh * 512:(fh + 1) * 512], ps_pv[:, fh, :]
                                ).then_inc(s2, 1)
        C["s2"] += NSG * 36

        @block.vector
        def _(ve):
            ve.wait_ge(sd, f_sd)
            for sgi in range(NSG):
                for st2 in range(2):
                    for dc in range(8):
                        i = st2 * 8 + dc
                        ve.wait_ge(tsm, f_ts + sgi * 52 + i + 1)
                        ve.tensor_copy(xdT[:, dc, st2 * 128:(st2 + 1) * 128],
                                       ps_pt[:, (i % 2) * 128:(i % 2) * 128 + 128])
                        ve.drain()
                        ve.engine_nop().then_inc(vs, 1)
                for ss in range(2):
                    t = sgi * 2 + ss
                    ve.wait_ge(s2, f_s2 + sgi * 36 + 32 + 2 * ss + 2)
                    if t >= 1:
                        ve.wait_ge(dro, t * 16)
                    ve.tensor_add(orowb[:, :], orow[:, :], b2s[:, :])
                    ve.drain()
                    ve.engine_nop().then_inc(vs, 1)
        C["vs"] += NSG * 18

        @block.gpsimd
        def _(g):
            for ss in range(2):
                t = (NSG - 1) * 2 + ss
                g.wait_ge(vs, f_vs + (NSG - 1) * 18 + 16 + ss + 1)
                r0 = (NSG - 1) * SG + ss * 128
                g.dma_start(out=ret_loc[r0:r0 + 128, :], in_=orowb[:, :]
                            ).then_inc(dro, 16)
            g.wait_ge(dro, NSG * 2 * 16)
            g.collective_compute("AllGather", OP.bypass,
                                 replica_groups=[list(range(NCORE))],
                                 ins=[ret_loc[:, :]], outs=[ret_all[0:NCORE * NSLOT, :]]
                                 ).then_inc(cc, 1)
        C["cc"] += 1
        cc_ret = C["cc"]

        g_vs, g_s2, g_sd = C["vs"], C["s2"], C["sd"]

        @block.sync
        def _(sy):
            sy.wait_ge(cc, cc_ret)
            sy.wait_ge(dgx, NT * 16)
            for t in range(NT):
                if t >= 1:
                    sy.wait_ge(vs, g_vs + t * 2)
                sy.dma_start(out=x1rl[:, :] if t % 2 == 0 else qres_b[:, 0, :],
                             in_=x1_spill[t, :, :]).then_inc(sd, 16)
        C["sd"] += NT * 16

        @block.scalar
        def _(sc):
            for t in range(NT):
                sc.wait_ge(vs, g_vs + t * 2 + 1)
                sc.activation(orow[:, :], yrow[:, :], ACT.Square, bias=0.0, scale=1.0,
                              accum_out=stat[:, 1:2])
                sc.drain()
                sc.activation(stat[:, 1:2], stat[:, 1:2], ACT.Sqrt, bias=epss[:, 0:1],
                              scale=1.0 / D).then_inc(s2, 1)
        C["s2"] += NT

        @block.vector
        def _(ve):
            for t in range(NT):
                ve.wait_ge(dgg if t % 2 == 0 else dgg2,
                           (t // 2 + 1) * 32)
                ve.wait_ge(sd, g_sd + (t + 1) * 16)
                if t >= 1:
                    ve.wait_ge(dgo, t * 16)
                xv = x1rl[:, :] if t % 2 == 0 else qres_b[:, 0, :]
                ve.tensor_scalar(yrow[:, :], ga0[:, t % 2, :], g1v[:, t:t + 1], None,
                                 OP.mult)
                ve.drain()
                ve.scalar_tensor_tensor(yrow[:, :], ga1[:, t % 2, :], g2v[:, t:t + 1],
                                        yrow[:, :], OP.mult, OP.add)
                ve.drain()
                ve.tensor_add(yrow[:, :], yrow[:, :], xv)
                ve.drain()
                ve.tensor_reduce(stat[:, 0:1], yrow[:, :], AXX, OP.add)
                ve.drain()
                ve.tensor_scalar(stat[:, 0:1], stat[:, 0:1], 1.0 / D, None, OP.mult)
                ve.drain()
                ve.tensor_scalar(yrow[:, :], yrow[:, :], stat[:, 0:1], None,
                                 OP.subtract)
                ve.drain()
                ve.engine_nop().then_inc(vs, 1)
                ve.wait_ge(s2, g_s2 + t + 1)
                ve.reciprocal(stat[:, 1:2], stat[:, 1:2])
                ve.drain()
                ve.tensor_scalar(yrow[:, :], yrow[:, :], stat[:, 1:2], None, OP.mult)
                ve.drain()
                ve.engine_nop().then_inc(vs, 1)
        C["vs"] += NT * 2

        @block.gpsimd
        def _(g):
            g.wait_ge(cc, cc_ret)
            for t in range(NT):
                if t >= 2:
                    g.wait_ge(vs, g_vs + (t - 1) * 2)
                g.indirect_dma_start(
                    out=ga0[:, t % 2, :], out_offset=None, in_=ret_all[:, :],
                    in_offset=IndirectOffsetOnAxis(ap=gidx[:, 0, t:t + 1], axis=0),
                    bounds_check=NCORE * NSLOT + 127, oob_is_err=False
                    ).then_inc(dgg if t % 2 == 0 else dgg2, 16)
                _rfree()
                g.indirect_dma_start(
                    out=ga1[:, t % 2, :], out_offset=None, in_=ret_all[:, :],
                    in_offset=IndirectOffsetOnAxis(ap=gidx[:, 1, t:t + 1], axis=0),
                    bounds_check=NCORE * NSLOT + 127, oob_is_err=False
                    ).then_inc(dgg if t % 2 == 0 else dgg2, 16)
                _rfree()
                if t >= 1:
                    g.wait_ge(vs, g_vs + t * 2)
                    g.dma_start(out=out[(t - 1) // 8, ((t - 1) % 8) * 128:
                                        ((t - 1) % 8) * 128 + 128, :],
                                in_=yrow[:, :]).then_inc(dgo, 16)
            g.wait_ge(vs, g_vs + NT * 2)
            g.dma_start(out=out[(NT - 1) // 8, ((NT - 1) % 8) * 128:
                                ((NT - 1) % 8) * 128 + 128, :],
                        in_=yrow[:, :]).then_inc(dgo, 16)
            g.wait_ge(dgo, NT * 16)

    return nc
    return nc


_NC_CACHE = [None]


def kernel(L0_attn_q, L0_attn_wq, L0_attn_wk, L0_attn_wv, L0_attn_wo,
           L0_ffn_Wr, L0_ffn_br, L0_ffn_W1, L0_ffn_b1, L0_ffn_W2, L0_ffn_b2):
    q = np.ascontiguousarray(np.asarray(L0_attn_q, dtype=np.float32).reshape(S, D))
    q3 = q.reshape(KI, 16, D)
    wqh = np.ascontiguousarray(np.asarray(L0_attn_wq, np.float32))
    wkh = np.ascontiguousarray(np.asarray(L0_attn_wk, np.float32))
    wvh = np.ascontiguousarray(np.asarray(L0_attn_wv, np.float32))
    woh = np.ascontiguousarray(np.asarray(L0_attn_wo, np.float32))
    wrh = np.ascontiguousarray(np.asarray(L0_ffn_Wr, np.float32))
    brh = np.asarray(L0_ffn_br, np.float32)
    w1h = np.asarray(L0_ffn_W1, np.float32)
    b1h = np.asarray(L0_ffn_b1, np.float32)
    w2h = np.asarray(L0_ffn_W2, np.float32)
    b2h = np.asarray(L0_ffn_b2, np.float32)

    if _NC_CACHE[0] is None:
        _NC_CACHE[0] = build_nc()
    nc = _NC_CACHE[0]

    brr = np.ascontiguousarray(np.broadcast_to(brh, (128, NE)))
    in_maps = []
    for c in range(NCORE):
        b1c = np.ascontiguousarray(b1h[c].reshape(HID // 128, 128).T)
        in_maps.append({
            "q_own": q[SOWN * c:SOWN * (c + 1)],
            "q_res": np.ascontiguousarray(q3[:, 2 * c:2 * c + 2, :].transpose(1, 0, 2)),
            "wq": wqh, "wk": wkh, "wv": wvh, "wo": woh,
            "wr": wrh, "brr": brr,
            "w1i": np.ascontiguousarray(w1h[c]),
            "b1i": b1c,
            "w2i": np.ascontiguousarray(w2h[c]),
            "b2i": np.ascontiguousarray(np.broadcast_to(b2h[c], (128, D))),
            "cvec": np.ascontiguousarray(
                np.broadcast_to(np.array([c, PAD * c, NSLOT * c, 0], np.float32),
                                (128, 4))),
        })
    res = run_bass_kernel_spmd(nc, in_maps, core_ids=list(range(NCORE)))
    outp = np.empty((KI, 16, D), np.float32)
    for c in range(NCORE):
        outp[:, 2 * c:2 * c + 2, :] = np.asarray(res.results[c]["out"]).reshape(2, KI, D).transpose(1, 0, 2)
    return outp.reshape(1, S, D)


# version 08 (final)
# speedup vs baseline: 1.0285x; 1.0075x over previous
import numpy as np
import ml_dtypes
import concourse.bass as bass
import concourse.mybir as mybir
from concourse.bass_core_rust import IndirectOffsetOnAxis
from concourse.bass_utils import run_bass_kernel_spmd

F32 = mybir.dt.float32
BF16 = mybir.dt.bfloat16
U32 = mybir.dt.uint32
OP = mybir.AluOpType
ACT = mybir.ActivationFunctionType
AXX = mybir.AxisListType.X

S = 16384
D = 1024
HID = 4096
NE = 8
NCORE = 8
SOWN = 2048
NT = 16
CAP = 2458
PAD = 448
NSLOT = NE * PAD
KB = 512
NKB = S // KB
NITER = 16
EPS = 1e-5
SG = 256
NSG = NSLOT // SG
KI = 1024
BIG = 1.0e6


def build_nc():
    nc = bass.Bass("TRN2", target_bir_lowering=False)
    ap = lambda t: t.ap()

    q_own = ap(nc.dram_tensor("q_own", [SOWN, D], F32, kind="ExternalInput"))
    q_res = ap(nc.dram_tensor("q_res", [2, KI, D], F32, kind="ExternalInput"))
    wq = ap(nc.dram_tensor("wq", [D, D], F32, kind="ExternalInput"))
    wk = ap(nc.dram_tensor("wk", [D, D], F32, kind="ExternalInput"))
    wv = ap(nc.dram_tensor("wv", [D, D], F32, kind="ExternalInput"))
    wo = ap(nc.dram_tensor("wo", [D, D], F32, kind="ExternalInput"))
    wr = ap(nc.dram_tensor("wr", [D, NE], F32, kind="ExternalInput"))
    brr = ap(nc.dram_tensor("brr", [128, NE], F32, kind="ExternalInput"))
    w1i = ap(nc.dram_tensor("w1i", [D, HID], F32, kind="ExternalInput"))
    b1i = ap(nc.dram_tensor("b1i", [128, HID // 128], F32, kind="ExternalInput"))
    w2i = ap(nc.dram_tensor("w2i", [HID, D], F32, kind="ExternalInput"))
    b2i = ap(nc.dram_tensor("b2i", [128, D], F32, kind="ExternalInput"))
    cvec = ap(nc.dram_tensor("cvec", [128, 4], F32, kind="ExternalInput"))
    out = ap(nc.dram_tensor("out", [2, KI, D], F32, kind="ExternalOutput"))

    idnb = ap(nc.inline_tensor(np.eye(128, dtype=np.float32).astype(ml_dtypes.bfloat16),
                               "c_idnb"))
    ones_c = ap(nc.inline_tensor(np.ones((128, 128), np.float32), "c_ones"))
    ltri_c = ap(nc.inline_tensor(
        (np.arange(128)[:, None] < np.arange(128)[None, :]).astype(np.float32), "c_ltri"))
    io8 = np.broadcast_to(np.arange(8, dtype=np.float32), (128, NT, 8)).reshape(128, 128)
    iota8_c = ap(nc.inline_tensor(io8.copy(), "c_iota8"))
    iota8m_c = ap(nc.inline_tensor((io8 - 8.0).copy(), "c_iota8m"))
    onesr_c = ap(nc.inline_tensor(np.ones((128, NT), np.float32), "c_onesr"))
    sglob = np.arange(NSLOT).reshape(NSLOT // 128, 128)
    fidx_i = ap(nc.inline_tensor((sglob // PAD * NSLOT).T.astype(np.float32).copy(),
                                 "c_fidxi"))
    fidx_r = ap(nc.inline_tensor((sglob % PAD).T.astype(np.float32).copy(), "c_fidxr"))
    pio_c = ap(nc.inline_tensor(np.arange(128, dtype=np.float32).reshape(128, 1).copy(),
                                "c_pio"))

    kv_k_loc = ap(nc.dram_tensor("kv_k_loc", [D, SOWN], BF16))
    kv_v_loc = ap(nc.dram_tensor("kv_v_loc", [SOWN, D], BF16))
    k_all = ap(nc.dram_tensor("k_all", [NCORE, D, SOWN], BF16))
    v_all = ap(nc.dram_tensor("v_all", [S, D], BF16))
    gates_loc = ap(nc.dram_tensor("gates_loc", [128, 64], F32))
    gates_all = ap(nc.dram_tensor("gates_all", [NCORE, 128, 64], F32))
    disp_loc = ap(nc.dram_tensor("disp_loc", [NSLOT + 32 * 128, D], BF16))
    disp_all = ap(nc.dram_tensor("disp_all", [NCORE * NSLOT, D], BF16))
    ret_loc = ap(nc.dram_tensor("ret_loc", [NSLOT, D], BF16))
    ret_all = ap(nc.dram_tensor("ret_all", [NCORE * NSLOT + 128, D], BF16))
    x1_spill = ap(nc.dram_tensor("x1_spill", [NT, 128, D], F32))

    K = 1024
    SB0 = 16512
    AT = lambda name, shape, dt, o: nc.alloc_sbuf_tensor_at(name, shape, dt,
                                                            offset=SB0 + o)
    qT = AT("qT", [128, 8, SOWN], BF16, 0)
    Ksb = AT("Ksb", [128, 8, SOWN], BF16, 32 * K)
    Osb = AT("Osb", [128, NT, D], F32, 0)
    w2s = AT("w2s", [128, 32, D], BF16, 0)
    Qf = AT("Qf", [128, 8, SOWN], BF16, 64 * K)
    zdisp = AT("zdisp", [128, NSLOT // 128, D], BF16, 64 * K)
    Vsb = AT("Vsb", [128, NT, D], BF16, 96 * K)
    Obf = AT("Obf", [128, NT, D], BF16, 96 * K)
    w1s = AT("w1s", [128, 8, HID], BF16, 64 * K)
    x1bf = AT("x1bf", [128, NT, D], BF16, 128 * K)
    hsb = AT("hsb", [128, 32, SG], BF16, 128 * K)
    xdT = AT("xdT", [128, 8, SG], BF16, 144 * K)
    xdisp = AT("xdisp", [128, 2, SG // 128, D], BF16, 152 * K)
    xdTC = AT("xdTC", [128, 8, 128], BF16, 0)
    cjunk = AT("cjunk", [128, D], F32, 4 * K)
    x1sc = AT("x1sc", [128, D], F32, 8 * K)
    x1lo = AT("x1lo", [128, D], BF16, 12 * K)
    xdTL = AT("xdTL", [128, 8, 128], BF16, 14 * K)
    wT = AT("wT", [128, 8, D], BF16, 160 * K)
    kbuf = AT("kbuf", [128, 2, 8, KB], BF16, 160 * K)
    orow = AT("orow", [128, D], F32, 160 * K)
    orowb = AT("orowb", [128, D], BF16, 164 * K)
    ga0 = AT("ga0", [128, 2, D], BF16, 166 * K)
    ga1 = AT("ga1", [128, 2, D], BF16, 8 * K)
    yrow = AT("yrow", [128, D], F32, 170 * K)
    x1rl = AT("x1rl", [128, D], F32, 174 * K)
    b2s = AT("b2s", [128, D], F32, 178 * K)
    vbuf = AT("vbuf", [128, 2, 4, D], BF16, 176 * K)
    wstage = AT("wstage", [128, 2, D], F32, 176 * K)
    wstgb = AT("wstgb", [128, 2, D], BF16, 184 * K)
    qres_b = AT("qres_b", [128, 2, D], F32, 176 * K)
    xstg = AT("xstg", [128, D], F32, 184 * K)
    off = [192 * K]

    def perm(name, shape, dt):
        t = AT(name, shape, dt, off[0])
        n = int(np.prod(shape[1:])) * mybir.dt.size(dt)
        off[0] += (n + 31) & ~31
        return t

    idn_s = perm("idn_s", [128, 128], BF16)
    ones_s = perm("ones_s", [128, 128], F32)
    ltri_s = perm("ltri_s", [128, 128], F32)
    iota8_s = perm("iota8_s", [128, NT, 8], F32)
    iota8m_s = perm("iota8m_s", [128, NT, 8], F32)
    onesr_s = perm("onesr_s", [128, NT], F32)
    fii_s = perm("fii_s", [128, NSLOT // 128], F32)
    fir_s = perm("fir_s", [128, NSLOT // 128], F32)
    fidxf_s = perm("fidxf_s", [128, NSLOT // 128], F32)
    fidx_s = perm("fidx_s", [128, NSLOT // 128], U32)
    cv = perm("cv", [128, 4], F32)
    wrs = perm("wrs", [128, 8, NE], BF16)
    brs = perm("brs", [128, NE], F32)
    b1s = perm("b1s", [128, HID // 128], F32)
    mst = perm("mst", [128, NT], F32)
    lsum = perm("lsum", [128, NT], F32)
    nmn = perm("nmn", [128, 2], F32)
    dmm = perm("dmm", [128, 2], F32)
    rmx = perm("rmx", [128, 1], F32)
    alp = perm("alp", [128, 2], F32)
    lbb = perm("lbb", [128, 2], F32)
    psb = perm("psb", [128, KB], BF16)
    ptb = perm("ptb", [128, KB], BF16)
    oinv = perm("oinv", [128, 1], F32)
    probs = perm("probs", [128, NT, NE], F32)
    scr8 = perm("scr8", [128, NT, NE], F32)
    scr8b = perm("scr8b", [128, NT, NE], F32)
    g1v = perm("g1v", [128, NT], F32)
    g2v = perm("g2v", [128, NT], F32)
    e1v = perm("e1v", [128, NT], F32)
    e2v = perm("e2v", [128, NT], F32)
    gall = perm("gall", [128, 8, 64], F32)
    mete = perm("mete", [128, 8, 8, 32], BF16)
    junkc = perm("junkc", [128, 256], BF16)
    thl = perm("thl", [128, NE], F32)
    thh = perm("thh", [128, NE], F32)
    thm = perm("thm", [128, NE], F32)
    cntp = perm("cntp", [128, NE], F32)
    cm1 = perm("cm1", [128, NE], F32)
    cm2 = perm("cm2", [128, NE], F32)
    ctt = perm("ctt", [128, NE], F32)
    thsel = perm("thsel", [128, NT], F32)
    vld = perm("vld", [128, 2, NT], F32)
    rnk = perm("rnk", [128, 2, NT], F32)
    rs8 = perm("rs8", [128, 2, NE], F32)
    off8 = perm("off8", [128, 2, NE], F32)
    tot8 = perm("tot8", [128, NE], F32)
    asc = perm("asc", [128, NT], F32)
    asc2 = perm("asc2", [128, NT], F32)
    asc3 = perm("asc3", [128, NT], F32)
    didxf = perm("didxf", [128, 2, NT], F32)
    didx = perm("didx", [128, 2, NT], U32)
    gidx = perm("gidx", [128, 2, NT], U32)
    stat = perm("stat", [128, 4], F32)
    lgts = perm("lgts", [128, NE], F32)
    epss = perm("epss", [128, 1], F32)
    wrf = perm("wrf", [128, NE, NE], F32)
    wrl = perm("wrl", [128, NE, NE], BF16)
    pio_s = perm("pio_s", [128, 1], F32)
    zrow_d = perm("zrow_d", [128, 1], F32)
    zrow_g = perm("zrow_g", [128, 1], F32)
    tio16 = perm("tio16", [128, NT], F32)
    zt16 = perm("zt16", [128, NT], F32)
    assert SB0 + off[0] <= 229344, (SB0, off[0])

    ps_s = nc.alloc_psum_tensor("ps_s", [128, KB], F32)
    ps_s2 = nc.alloc_psum_tensor("ps_s2", [128, KB], F32)
    ps_pt = nc.alloc_psum_tensor("ps_pt", [128, KB], BF16)
    ps_pv = nc.alloc_psum_tensor("ps_pv", [128, 2, KB], F32)
    ps_m = nc.alloc_psum_tensor("ps_m", [128, 2, KB], F32)
    ps_t = nc.alloc_psum_tensor("ps_t", [128, 128], F32)

    sd = nc.alloc_semaphore("sd")
    sdc = nc.alloc_semaphore("sdc")
    sdl0 = nc.alloc_semaphore("sdl0")
    sdl1 = nc.alloc_semaphore("sdl1")
    sdk0 = nc.alloc_semaphore("sdk0")
    sdk1 = nc.alloc_semaphore("sdk1")
    sq0 = nc.alloc_semaphore("sq0")
    sq1 = nc.alloc_semaphore("sq1")
    dg = nc.alloc_semaphore("dg")
    dgx = nc.alloc_semaphore("dgx")
    dgz = nc.alloc_semaphore("dgz")
    dgx2a = nc.alloc_semaphore("dgx2a")
    dgx2b = nc.alloc_semaphore("dgx2b")
    dro = nc.alloc_semaphore("dro")
    dgg = nc.alloc_semaphore("dgg")
    dgg2 = nc.alloc_semaphore("dgg2")
    dgo = nc.alloc_semaphore("dgo")
    cc = nc.alloc_semaphore("cc")
    vs = nc.alloc_semaphore("vs")
    s2 = nc.alloc_semaphore("s2")
    tsm = nc.alloc_semaphore("ts")
    with nc.Block() as block:
        sdl = [sdl0, sdl1]
        sdk = [sdk0, sdk1]
        sq = [sq0, sq1]
        dgx2 = [dgx2a, dgx2b]
        C = {"sd": 0, "dg": 0, "cc": 0, "vs": 0, "s2": 0, "ts": 0}
        POOL = mybir.EngineType.Pool
        _rind = [nc.alloc_register(POOL, "rind%d" % i) for i in range(3)]
        for _r in _rind:
            nc.free_register(_r)

        def _rfree():
            for _r in _rind:
                nc.free_register(_r)

        @block.sync
        def _(sy):
            sy.dma_start(out=idn_s[:, :], in_=idnb[:, :]).then_inc(sdc, 16)
            sy.dma_start(out=ones_s[:, :], in_=ones_c[:, :]).then_inc(sdc, 16)
            sy.dma_start(out=ltri_s[:, :], in_=ltri_c[:, :]).then_inc(sdc, 16)
            sy.dma_start(out=iota8_s.rearrange("p a e -> p (a e)")[:, :],
                         in_=iota8_c[:, :]).then_inc(sdc, 16)
            sy.dma_start(out=iota8m_s.rearrange("p a e -> p (a e)")[:, :],
                         in_=iota8m_c[:, :]).then_inc(sdc, 16)
            sy.dma_start(out=onesr_s[:, :], in_=onesr_c[:, :]).then_inc(sdc, 16)
            sy.dma_start(out=fii_s[:, :], in_=fidx_i[:, :]).then_inc(sdc, 16)
            sy.dma_start(out=fir_s[:, :], in_=fidx_r[:, :]).then_inc(sdc, 16)
            sy.dma_start(out=cv[:, :], in_=cvec[:, :]).then_inc(sdc, 16)
            sy.dma_start(out=brs[:, :], in_=brr[:, :]).then_inc(sdc, 16)
            sy.dma_start(out=b1s[:, :], in_=b1i[:, :]).then_inc(sdc, 16)
            sy.dma_start(out=wrf[:, :, :],
                         in_=wr.rearrange("(dc p) e -> p dc e", p=128)[:, :, :]
                         ).then_inc(sdc, 16)
            sy.dma_start(out=pio_s[:, :], in_=pio_c[:, :]).then_inc(sdc, 16)
        NCONST = 13 * 16


        LT = {"n": [0, 0]}

        def load_T(src, ntile, dest_fn, gate=None):
            b_s2, b_ts, b_vs = C["s2"], C["ts"], C["vs"]
            b_lt = list(LT["n"])

            @block.sync
            def _(sy):
                if gate is not None:
                    sy.wait_ge(gate[0], gate[1])
                sy.wait_ge(dg, 16)
                sy.wait_ge(s2, b_s2)
                for t in range(ntile):
                    if t >= 2:
                        sy.wait_ge(s2, b_s2 + t - 1)
                    sy.dma_start(out=wstage[:, t % 2, :],
                                 in_=src[t * 128:(t + 1) * 128, :]
                                 ).then_inc(sdl[t % 2], 16)

            @block.scalar
            def _(sc):
                sc.wait_ge(tsm, b_ts)
                for t in range(ntile):
                    sc.wait_ge(sdl[t % 2], (b_lt[t % 2] + t // 2 + 1) * 16)
                    if t >= 2:
                        sc.wait_ge(tsm, b_ts + (t - 1) * 8)
                    sc.activation(wstgb[:, t % 2, :], wstage[:, t % 2, :], ACT.Copy,
                                  bias=0.0, scale=1.0).then_inc(s2, 1)
            LT["n"][0] += (ntile + 1) // 2
            LT["n"][1] += ntile // 2

            @block.tensor
            def _(te):
                te.wait_ge(sdc, NCONST)
                te.wait_ge(vs, b_vs)
                for t in range(ntile):
                    te.wait_ge(s2, b_s2 + t + 1)
                    for dc in range(8):
                        i = t * 8 + dc
                        if i >= 2:
                            te.wait_ge(vs, b_vs + i - 1)
                        te.transpose(ps_pt[:, (i % 2) * 128:(i % 2) * 128 + 128],
                                     wstgb[:, t % 2, dc * 128:(dc + 1) * 128],
                                     idn_s[:, :]).then_inc(tsm, 1)

            @block.vector
            def _(ve):
                for t in range(ntile):
                    for dc in range(8):
                        i = t * 8 + dc
                        ve.wait_ge(tsm, b_ts + i + 1)
                        ve.tensor_copy(dest_fn(t, dc),
                                       ps_pt[:, (i % 2) * 128:(i % 2) * 128 + 128])
                        ve.drain()
                        ve.engine_nop().then_inc(vs, 1)

            C["s2"] += ntile
            C["ts"] += ntile * 8
            C["vs"] += ntile * 8

        @block.vector
        def _(ve):
            ve.memset(zdisp[:, :, :], 0.0)
            ve.drain()
            ve.engine_nop().then_inc(vs, 1)
        C["vs"] += 1

        @block.gpsimd
        def _(g):
            g.wait_ge(vs, C["vs"])
            for zc in range(4):
                g.dma_start(out=disp_loc.rearrange("(a p) d -> p a d", p=128)[:, zc * 7:(zc + 1) * 7, :],
                            in_=zdisp[:, zc * 7:(zc + 1) * 7, :]).then_inc(dgz, 16)
            g.dma_start(out=ret_all[NCORE * NSLOT:NCORE * NSLOT + 128, :],
                        in_=zdisp[:, 0, :]).then_inc(dgz, 16)
            g.dma_start(out=wrs[:, :, :],
                        in_=wr.rearrange("(dc p) e -> p dc e", p=128)[:, :, :]
                        ).then_inc(dg, 16)
        DGZ = 80
        C["dg"] += 16

        load_T(q_own, NT, lambda t, dc: qT[:, dc, t * 128:(t + 1) * 128],
               gate=(dgz, DGZ))

        def proj(wsrc, kind):
            load_T(wsrc, 8, lambda t, dc: wT[:, dc, t * 128:(t + 1) * 128])
            b_ts, b_s2, b_vs = C["ts"], C["s2"], C["vs"]
            if kind in ("K", "Q"):
                dest = Ksb if kind == "K" else Qf
                scl = 0.125 if kind == "Q" else 1.0

                @block.tensor
                def _(te):
                    te.wait_ge(vs, b_vs)
                    n = 0
                    for ec in range(8):
                        for sh in range(4):
                            n += 1
                            if n > 2:
                                te.wait_ge(s2, b_s2 + n - 2)
                            for dc in range(8):
                                _mi = te.matmul(ps_m[:, n % 2, :],
                                                wT[:, dc, ec * 128:(ec + 1) * 128],
                                                qT[:, dc, sh * 512:(sh + 1) * 512],
                                                start=(dc == 0), stop=(dc == 7))
                            _mi.then_inc(tsm, 1)

                @block.scalar
                def _(sc):
                    if kind == "Q":
                        sc.nop()
                    n = 0
                    for ec in range(8):
                        for sh in range(4):
                            n += 1
                            sc.wait_ge(tsm, b_ts + n)
                            sc.activation(dest[:, ec, sh * 512:(sh + 1) * 512],
                                          ps_m[:, n % 2, :], ACT.Copy, bias=0.0,
                                          scale=scl).then_inc(s2, 1)
            else:
                @block.tensor
                def _(te):
                    te.wait_ge(vs, b_vs)
                    n = 0
                    for st in range(NT):
                        for eh in range(2):
                            n += 1
                            if n > 2:
                                te.wait_ge(s2, b_s2 + n - 2)
                            for dc in range(8):
                                _mi = te.matmul(ps_m[:, n % 2, :],
                                                qT[:, dc, st * 128:(st + 1) * 128],
                                                wT[:, dc, eh * 512:(eh + 1) * 512],
                                                start=(dc == 0), stop=(dc == 7))
                            _mi.then_inc(tsm, 1)

                @block.scalar
                def _(sc):
                    sc.nop()
                    n = 0
                    for st in range(NT):
                        for eh in range(2):
                            n += 1
                            sc.wait_ge(tsm, b_ts + n)
                            sc.activation(Vsb[:, st, eh * 512:(eh + 1) * 512],
                                          ps_m[:, n % 2, :], ACT.Copy, bias=0.0,
                                          scale=1.0).then_inc(s2, 1)
            C["ts"] += 32
            C["s2"] += 32

        proj(wk, "K")
        proj(wv, "V")
        proj(wq, "Q")

        dg_a3 = C["dg"]

        @block.gpsimd
        def _(g):
            g.wait_ge(s2, C["s2"])
            for zc in range(2):
                g.dma_start(out=kv_k_loc.rearrange("(c p) s -> p c s", p=128)[:, zc * 4:(zc + 1) * 4, :],
                            in_=Ksb[:, zc * 4:(zc + 1) * 4, :]).then_inc(dg, 16)
                g.dma_start(out=kv_v_loc.rearrange("(t p) e -> p t e", p=128)[:, zc * 8:(zc + 1) * 8, :],
                            in_=Vsb[:, zc * 8:(zc + 1) * 8, :]).then_inc(dg, 16)
            g.wait_ge(dg, dg_a3 + 64)
            g.collective_compute("AllGather", OP.bypass,
                                 replica_groups=[list(range(NCORE))],
                                 ins=[kv_k_loc[:, :]], outs=[k_all[:, :, :]]
                                 ).then_inc(cc, 1)
            g.collective_compute("AllGather", OP.bypass,
                                 replica_groups=[list(range(NCORE))],
                                 ins=[kv_v_loc[:, :]], outs=[v_all[:, :]]
                                 ).then_inc(cc, 1)
        C["dg"] += 64
        C["cc"] += 2

        @block.vector
        def _(ve):
            ve.wait_ge(tsm, C["ts"])
            ve.wait_ge(dg, dg_a3 + 64)
            ve.memset(mst[:, :], -3.0e38)
            ve.memset(lsum[:, :], 0.0)
            ve.memset(epss[:, :], EPS)
            ve.memset(Osb[:, :, :], 0.0)
            ve.drain()
            ve.engine_nop().then_inc(vs, 1)
        C["vs"] += 1

        b_ts, b_vs, b_s2 = C["ts"], C["vs"], C["s2"]
        NIT = NKB * NT

        @block.sync
        def _(sy):
            sy.wait_ge(cc, C["cc"])
            for kb in range(NKB):
                if kb >= 2:
                    sy.wait_ge(tsm, b_ts + 3 * NT * (kb - 1) + 1)
                r, s0 = kb // 4, (kb % 4) * KB
                sy.dma_start(out=kbuf[:, kb % 2, :, :],
                             in_=k_all[r, :, s0:s0 + KB].rearrange(
                                 "(c p) s -> p c s", p=128)[:, :, :]
                             ).then_inc(sdk[kb % 2], 16)
                sy.dma_start(out=vbuf[:, kb % 2, :, :],
                             in_=v_all[kb * KB:(kb + 1) * KB, :].rearrange(
                                 "(c p) e -> p c e", p=128)[:, :, :]
                             ).then_inc(sdk[kb % 2], 16)

        def s_cum(j):
            return 1 if j == 0 else 3 * j - 1

        def t_cum(it):
            return 3 * it + 3 if it < NIT - 1 else 3 * it + 2

        def pv_cum(it):
            return 3 * it + 4 if it < NIT - 1 else 3 * it + 3

        @block.tensor
        def _(te):
            def sgrp(j):
                kbj, qtj = j // NT, j % NT
                sN = ps_s if j % 2 == 0 else ps_s2
                if qtj == 0:
                    te.wait_ge(sdk[kbj % 2], (kbj // 2 + 1) * 32)
                if j >= 2:
                    te.wait_ge(s2, b_s2 + 3 * (j - 2) + 1)
                for ec in range(8):
                    _mi = te.matmul(sN[:, :], Qf[:, ec, qtj * 128:(qtj + 1) * 128],
                                    kbuf[:, kbj % 2, ec, :],
                                    start=(ec == 0), stop=(ec == 7))
                _mi.then_inc(tsm, 1)
            sgrp(0)
            for kb in range(NKB):
                for qt in range(NT):
                    it = kb * NT + qt
                    if it + 1 < NIT:
                        sgrp(it + 1)
                    te.wait_ge(s2, b_s2 + 3 * it + 1)
                    for kc in range(4):
                        _mi = te.transpose(ps_pt[:, kc * 128:(kc + 1) * 128],
                                           psb[:, kc * 128:(kc + 1) * 128], idn_s[:, :])
                    _mi.then_inc(tsm, 1)
                    te.wait_ge(s2, b_s2 + 3 * it + 3)
                    if it >= 1:
                        te.wait_ge(vs, b_vs + 2 * it)
                    for kc in range(4):
                        for eh in range(2):
                            _mi = te.matmul(ps_pv[:, eh, :],
                                            ptb[:, kc * 128:(kc + 1) * 128],
                                            vbuf[:, kb % 2, kc, eh * 512:(eh + 1) * 512],
                                            start=(kc == 0), stop=(kc == 3))
                    _mi.then_inc(tsm, 1)
        C["ts"] += NIT * 3

        @block.vector
        def _(ve):
            for kb in range(NKB):
                for qt in range(NT):
                    it = kb * NT + qt
                    sN = ps_s if it % 2 == 0 else ps_s2
                    ve.wait_ge(tsm, b_ts + s_cum(it))
                    if it >= 2:
                        ve.wait_ge(s2, b_s2 + 3 * (it - 2) + 2)
                    ve.tensor_reduce(rmx[:, :], sN[:, :], AXX, OP.max)
                    ve.drain()
                    ve.tensor_max(rmx[:, :], rmx[:, :], mst[:, qt:qt + 1])
                    ve.drain()
                    ve.tensor_scalar(nmn[:, it % 2:it % 2 + 1], rmx[:, :], -1.0, None,
                                     OP.mult)
                    ve.tensor_sub(dmm[:, it % 2:it % 2 + 1], mst[:, qt:qt + 1],
                                  rmx[:, :])
                    ve.drain()
                    ve.tensor_copy(mst[:, qt:qt + 1], rmx[:, :])
                    ve.drain()
                    ve.engine_nop().then_inc(vs, 1)
                    ve.wait_ge(tsm, b_ts + pv_cum(it))
                    ve.wait_ge(s2, b_s2 + 3 * it + 2)
                    for eh in range(2):
                        ve.scalar_tensor_tensor(
                            Osb[:, qt, eh * 512:(eh + 1) * 512],
                            Osb[:, qt, eh * 512:(eh + 1) * 512],
                            alp[:, it % 2:it % 2 + 1], ps_pv[:, eh, :],
                            OP.mult, OP.add)
                    ve.drain()
                    ve.scalar_tensor_tensor(lsum[:, qt:qt + 1], lsum[:, qt:qt + 1],
                                            alp[:, it % 2:it % 2 + 1],
                                            lbb[:, it % 2:it % 2 + 1], OP.mult, OP.add)
                    ve.drain()
                    ve.engine_nop().then_inc(vs, 1)
        C["vs"] += NIT * 2

        @block.scalar
        def _(sc):
            for kb in range(NKB):
                for qt in range(NT):
                    it = kb * NT + qt
                    sN = ps_s if it % 2 == 0 else ps_s2
                    sc.wait_ge(vs, b_vs + 2 * it + 1)
                    if it >= 2:
                        sc.wait_ge(vs, b_vs + 2 * (it - 2) + 2)
                    if it >= 1:
                        sc.wait_ge(tsm, b_ts + t_cum(it - 1))
                    sc.activation(psb[:, :], sN[:, :], ACT.Exp,
                                  bias=nmn[:, it % 2:it % 2 + 1], scale=1.0,
                                  accum_out=lbb[:, it % 2:it % 2 + 1]).then_inc(s2, 1)
                    sc.activation(alp[:, it % 2:it % 2 + 1], dmm[:, it % 2:it % 2 + 1],
                                  ACT.Exp, bias=0.0, scale=1.0).then_inc(s2, 1)
                    sc.wait_ge(tsm, b_ts + t_cum(it))
                    sc.activation(ptb[:, :], ps_pt[:, :], ACT.Copy, bias=0.0,
                                  scale=1.0).then_inc(s2, 1)
        C["s2"] += NIT * 3

        @block.vector
        def _(ve):
            for t in range(NT):
                ve.reciprocal(oinv[:, :], lsum[:, t:t + 1])
                ve.drain()
                ve.tensor_scalar(Obf[:, t, :], Osb[:, t, :], oinv[:, 0:1], None, OP.mult)
                ve.drain()
            ve.engine_nop().then_inc(vs, 1)
        C["vs"] += 1
        vs_obf = C["vs"]

        load_T(wo, 8, lambda t, dc: wT[:, dc, t * 128:(t + 1) * 128],
               gate=(vs, vs_obf))
        c_ts, c_vs, c_s2 = C["ts"], C["vs"], C["s2"]

        @block.sync
        def _(sy):
            sy.wait_ge(s2, c_s2)
            for t in range(NT):
                j, m = t // 8, t % 8
                if t >= 2:
                    sy.wait_ge(vs, c_vs + (t - 2) * 6 + 1)
                sy.dma_start(out=qres_b[:, t % 2, :],
                             in_=q_res[j, m * 128:(m + 1) * 128, :]
                             ).then_inc(sq[t % 2], 16)

        @block.tensor
        def _(te):
            te.wait_ge(vs, c_vs)
            for t in range(NT):
                j, m = t // 8, t % 8
                if t >= 1:
                    te.wait_ge(vs, c_vs + (t - 1) * 6 + 1)
                for fh in range(2):
                    for c8 in range(8):
                        _mi = te.matmul(ps_m[:, fh, :],
                                        Obf[:, 8 * j + c8, m * 128:(m + 1) * 128],
                                        wT[:, c8, fh * 512:(fh + 1) * 512],
                                        start=(c8 == 0), stop=(c8 == 7))
                _mi.then_inc(tsm, 1)
                te.wait_ge(vs, c_vs + t * 6 + 2)
                if t >= 1:
                    te.wait_ge(s2, c_s2 + (t - 1) * 4 + 3)
                for dc in range(4):
                    _mi = te.transpose(ps_pt[:, dc * 128:(dc + 1) * 128],
                                       x1bf[:, t, dc * 128:(dc + 1) * 128], idn_s[:, :])
                _mi.then_inc(tsm, 1)
                te.wait_ge(s2, c_s2 + t * 4 + 2)
                for dc in range(4, 8):
                    _mi = te.transpose(ps_pt[:, (dc - 4) * 128:(dc - 3) * 128],
                                       x1bf[:, t, dc * 128:(dc + 1) * 128], idn_s[:, :])
                _mi.then_inc(tsm, 1)
                te.wait_ge(s2, c_s2 + t * 4 + 3)
                for dc in range(4):
                    _mi = te.transpose(ps_pt[:, dc * 128:(dc + 1) * 128],
                                       x1lo[:, dc * 128:(dc + 1) * 128], idn_s[:, :])
                _mi.then_inc(tsm, 1)
                te.wait_ge(vs, c_vs + t * 6 + 3)
                for dc in range(4, 8):
                    _mi = te.transpose(ps_pt[:, (dc - 4) * 128:(dc - 3) * 128],
                                       x1lo[:, dc * 128:(dc + 1) * 128], idn_s[:, :])
                _mi.then_inc(tsm, 1)
                te.wait_ge(vs, c_vs + t * 6 + 4)
                if t >= 1:
                    te.wait_ge(vs, c_vs + (t - 1) * 6 + 5)
                for pp in range(3):
                    for dc in range(8):
                        lh = xdTC if pp != 1 else xdTL
                        wh = wrs if pp != 2 else wrl
                        _mi = te.matmul(ps_t[:, 0:8], lh[:, dc, :], wh[:, dc, :],
                                        start=(pp == 0 and dc == 0),
                                        stop=(pp == 2 and dc == 7))
                _mi.then_inc(tsm, 1)
        C["ts"] += NT * 6

        @block.scalar
        def _(sc):
            for t in range(NT):
                sc.wait_ge(vs, c_vs + t * 6 + 1)
                sc.activation(cjunk[:, :], xstg[:, :], ACT.Square, bias=0.0, scale=1.0,
                              accum_out=stat[:, 1:2])
                sc.drain()
                sc.activation(stat[:, 1:2], stat[:, 1:2], ACT.Sqrt, bias=epss[:, 0:1],
                              scale=1.0 / D).then_inc(s2, 1)
                sc.wait_ge(tsm, c_ts + t * 6 + 2)
                for dc in range(4):
                    _si = sc.copy(xdTC[:, dc, :], ps_pt[:, dc * 128:(dc + 1) * 128])
                _si.then_inc(s2, 1)
                sc.wait_ge(tsm, c_ts + t * 6 + 3)
                for dc in range(4, 8):
                    _si = sc.copy(xdTC[:, dc, :],
                                  ps_pt[:, (dc - 4) * 128:(dc - 3) * 128])
                _si.then_inc(s2, 1)
                sc.wait_ge(vs, c_vs + t * 6 + 5)
                sc.activation(probs[:, t, :], lgts[:, :], ACT.Exp, bias=stat[:, 2:3],
                              scale=1.0, accum_out=stat[:, 3:4]).then_inc(s2, 1)
        C["s2"] += NT * 4

        @block.vector
        def _(ve):
            ve.wait_ge(sdc, NCONST)
            ve.wait_ge(dg, 16)
            ve.tensor_copy(cjunk[:, 0:64],
                           wrs.rearrange("p a e -> p (a e)")[:, :])
            ve.drain()
            ve.tensor_tensor(wrl.rearrange("p a e -> p (a e)")[:, :],
                             wrf.rearrange("p a e -> p (a e)")[:, :],
                             cjunk[:, 0:64], OP.subtract)
            ve.drain()
            for t in range(NT):
                ve.wait_ge(tsm, c_ts + t * 6 + 1)
                ve.wait_ge(sq[t % 2], (t // 2 + 1) * 16)
                if t >= 1:
                    ve.wait_ge(dgx, t * 16)
                if t >= 2:
                    ve.wait_ge(s2, c_s2 + (t - 2) * 4 + 4)
                for fh in range(2):
                    ve.tensor_add(xstg[:, fh * 512:(fh + 1) * 512],
                                  qres_b[:, t % 2, fh * 512:(fh + 1) * 512],
                                  ps_m[:, fh, :])
                ve.drain()
                ve.tensor_reduce(stat[:, 0:1], xstg[:, :], AXX, OP.add)
                ve.drain()
                ve.tensor_scalar(stat[:, 0:1], stat[:, 0:1], 1.0 / D, None, OP.mult)
                ve.drain()
                ve.tensor_scalar(xstg[:, :], xstg[:, :], stat[:, 0:1], None, OP.subtract)
                ve.drain()
                ve.engine_nop().then_inc(vs, 1)
                ve.wait_ge(s2, c_s2 + t * 4 + 1)
                ve.reciprocal(stat[:, 1:2], stat[:, 1:2])
                ve.drain()
                ve.tensor_scalar(x1bf[:, t, :], xstg[:, :], stat[:, 1:2], None, OP.mult)
                ve.drain()
                ve.tensor_scalar(xstg[:, :], xstg[:, :], stat[:, 1:2], None, OP.mult)
                ve.drain()
                ve.tensor_copy(x1sc[:, :], x1bf[:, t, :])
                ve.drain()
                ve.tensor_tensor(x1lo[:, :], xstg[:, :], x1sc[:, :], OP.subtract)
                ve.drain()
                ve.engine_nop().then_inc(vs, 1)
                ve.wait_ge(tsm, c_ts + t * 6 + 4)
                for dc in range(4):
                    ve.tensor_copy(xdTL[:, dc, :], ps_pt[:, dc * 128:(dc + 1) * 128])
                ve.drain()
                ve.engine_nop().then_inc(vs, 1)
                ve.wait_ge(tsm, c_ts + t * 6 + 5)
                for dc in range(4, 8):
                    ve.tensor_copy(xdTL[:, dc, :],
                                   ps_pt[:, (dc - 4) * 128:(dc - 3) * 128])
                ve.drain()
                ve.engine_nop().then_inc(vs, 1)
                ve.wait_ge(tsm, c_ts + t * 6 + 6)
                ve.tensor_add(lgts[:, :], ps_t[:, 0:8], brs[:, :])
                ve.drain()
                ve.tensor_reduce(stat[:, 2:3], lgts[:, :], AXX, OP.max, negate=True)
                ve.drain()
                ve.engine_nop().then_inc(vs, 1)
                ve.wait_ge(s2, c_s2 + t * 4 + 4)
                ve.reciprocal(stat[:, 3:4], stat[:, 3:4])
                ve.drain()
                ve.tensor_scalar(probs[:, t, :], probs[:, t, :], stat[:, 3:4], None,
                                 OP.mult)
                ve.drain()
                ve.engine_nop().then_inc(vs, 1)
        C["vs"] += NT * 6

        @block.gpsimd
        def _(g):
            for t in range(NT):
                g.wait_ge(vs, c_vs + t * 6 + 2)
                g.dma_start(out=x1_spill[t, :, :], in_=xstg[:, :]).then_inc(dgx, 16)

        d_vs, d_s2, d_ts, d_dg, d_cc, d_sd = (C["vs"], C["s2"], C["ts"], C["dg"],
                                              C["cc"], C["sd"])
        pr3 = probs
        i83 = iota8_s
        i8m3 = iota8m_s

        def bc(v):
            return v.rearrange("p (t o) -> p t o", o=1).broadcast_to([128, NT, 8])

        @block.vector
        def _(ve):
            ve.wait_ge(vs, d_vs)
            ve.tensor_reduce(g1v[:, :], pr3[:, :, :], AXX, OP.max)
            ve.drain()
            ve.tensor_tensor(scr8[:, :, :], pr3[:, :, :], bc(g1v), OP.is_equal)
            ve.drain()
            ve.tensor_tensor(scr8b[:, :, :], i8m3[:, :, :], scr8[:, :, :], OP.mult)
            ve.drain()
            ve.tensor_scalar(scr8b[:, :, :], scr8b[:, :, :], 8.0, None, OP.add)
            ve.drain()
            ve.tensor_reduce(e1v[:, :], scr8b[:, :, :], AXX, OP.min)
            ve.drain()
            ve.tensor_tensor(scr8[:, :, :], i83[:, :, :], bc(e1v), OP.is_equal)
            ve.drain()
            ve.scalar_tensor_tensor(scr8[:, :, :], scr8[:, :, :], -1.0e30,
                                    pr3[:, :, :], OP.mult, OP.add)
            ve.drain()
            ve.tensor_reduce(g2v[:, :], scr8[:, :, :], AXX, OP.max)
            ve.drain()
            ve.tensor_tensor(scr8b[:, :, :], scr8[:, :, :], bc(g2v), OP.is_equal)
            ve.drain()
            ve.tensor_tensor(scr8b[:, :, :], i8m3[:, :, :], scr8b[:, :, :], OP.mult)
            ve.drain()
            ve.tensor_scalar(scr8b[:, :, :], scr8b[:, :, :], 8.0, None, OP.add)
            ve.drain()
            ve.tensor_reduce(e2v[:, :], scr8b[:, :, :], AXX, OP.min)
            ve.drain()
            ve.engine_nop().then_inc(vs, 1)
        C["vs"] += 1

        @block.gpsimd
        def _(g):
            g.wait_ge(vs, C["vs"])
            g.dma_start(out=gates_loc[:, 0:16], in_=g1v[:, :]).then_inc(dg, 16)
            g.dma_start(out=gates_loc[:, 16:32], in_=g2v[:, :]).then_inc(dg, 16)
            g.dma_start(out=gates_loc[:, 32:48], in_=e1v[:, :]).then_inc(dg, 16)
            g.dma_start(out=gates_loc[:, 48:64], in_=e2v[:, :]).then_inc(dg, 16)
            g.wait_ge(dg, C["dg"] + 64)
            g.collective_compute("AllGather", OP.bypass,
                                 replica_groups=[list(range(NCORE))],
                                 ins=[gates_loc[:, :]], outs=[gates_all[:, :, :]]
                                 ).then_inc(cc, 1)
        C["dg"] += 64
        C["cc"] += 1

        @block.sync
        def _(sy):
            sy.wait_ge(cc, C["cc"])
            sy.dma_start(out=gall[:, :, :],
                         in_=gates_all.rearrange("i p f -> p i f")[:, :, :]
                         ).then_inc(sd, 16)
        C["sd"] += 16
        sd_gall = C["sd"]

        gfl = gall[:, :, 0:32]
        ifl = gall[:, :, 32:64]

        bis_vs, bis_ts = C["vs"], C["ts"]

        @block.vector
        def _(ve):
            ve.wait_ge(sd, sd_gall)
            for e in range(NE):
                ve.tensor_scalar(mete[:, e, :, :], ifl, float(e), None, OP.is_equal)
            ve.drain()
            ve.memset(thl[:, :], 0.0)
            ve.memset(thh[:, :], 1.0)
            ve.memset(thm[:, :], 0.5)
            ve.drain()
            ve.engine_nop().then_inc(vs, 1)
            for i in range(NITER):
                for e in range(NE):
                    ve.scalar_tensor_tensor(junkc.rearrange("p (i f) -> p i f", i=8),
                                            gfl, thm[:, e:e + 1],
                                            mete[:, e, :, :], OP.is_gt, OP.mult,
                                            accum_out=cntp[:, e:e + 1])
                    ve.drain()
                ve.engine_nop().then_inc(vs, 1)
                ve.wait_ge(tsm, bis_ts + i + 1)
                ve.tensor_scalar(cm1[:, :], ps_t[:, 0:8], float(CAP), None, OP.is_gt)
                ve.drain()
                ve.tensor_scalar(cm2[:, :], cm1[:, :], -1.0, 1.0, OP.mult, OP.add)
                ve.drain()
                ve.tensor_sub(ctt[:, :], thm[:, :], thl[:, :])
                ve.drain()
                ve.tensor_mul(ctt[:, :], ctt[:, :], cm1[:, :])
                ve.drain()
                ve.tensor_add(thl[:, :], thl[:, :], ctt[:, :])
                ve.drain()
                ve.tensor_sub(ctt[:, :], thm[:, :], thh[:, :])
                ve.drain()
                ve.tensor_mul(ctt[:, :], ctt[:, :], cm2[:, :])
                ve.drain()
                ve.tensor_add(thh[:, :], thh[:, :], ctt[:, :])
                ve.drain()
                ve.tensor_add(thm[:, :], thl[:, :], thh[:, :])
                ve.drain()
                ve.tensor_scalar(thm[:, :], thm[:, :], 0.5, None, OP.mult)
                ve.drain()
            ve.engine_nop().then_inc(vs, 1)
        C["vs"] += 2 + NITER

        @block.tensor
        def _(te):
            for i in range(NITER):
                te.wait_ge(vs, bis_vs + 1 + i + 1)
                te.matmul(ps_t[:, 0:8], ones_s[:, :], cntp[:, 0:8],
                          start=True, stop=True).then_inc(tsm, 1)
        C["ts"] += NITER

        r_vs, r_ts = C["vs"], C["ts"]

        @block.vector
        def _(ve):
            ve.memset(rnk[:, 0, :], 0.0)
            ve.memset(rnk[:, 1, :], 0.0)
            ve.drain()
            for j in range(2):
                ev = e1v if j == 0 else e2v
                gv = g1v if j == 0 else g2v
                ve.memset(thsel[:, :], 0.0)
                ve.drain()
                for e in range(NE):
                    ve.tensor_scalar(asc[:, :], ev[:, :], float(e), None, OP.is_equal)
                    ve.drain()
                    ve.scalar_tensor_tensor(thsel[:, :], asc[:, :], thh[:, e:e + 1],
                                            thsel[:, :], OP.mult, OP.add)
                    ve.drain()
                ve.tensor_tensor(vld[:, j, :], gv[:, :], thsel[:, :], OP.is_gt)
                ve.drain()
                for e in range(NE):
                    ve.tensor_scalar(asc[:, :], ev[:, :], float(e), None, OP.is_equal)
                    ve.drain()
                    ve.tensor_mul(asc[:, :], asc[:, :], vld[:, j, :])
                    ve.drain()
                    ve.tensor_reduce(rs8[:, j, e:e + 1], asc[:, :], AXX, OP.add)
                    ve.drain()
            ve.engine_nop().then_inc(vs, 1)
        C["vs"] += 1

        rk_vs, rk_ts = C["vs"], C["ts"]

        @block.tensor
        def _(te):
            te.wait_ge(vs, rk_vs)
            te.matmul(ps_t[:, 0:8], ltri_s[:, :], rs8[:, 0, 0:8], start=True,
                      stop=True).then_inc(tsm, 1)
            te.matmul(ps_m[:, 0, 0:8], ltri_s[:, :], rs8[:, 1, 0:8], start=True,
                      stop=True).then_inc(tsm, 1)
            te.matmul(ps_m[:, 1, 0:8], ones_s[:, :], rs8[:, 0, 0:8], start=True,
                      stop=True).then_inc(tsm, 1)
        C["ts"] += 3

        @block.vector
        def _(ve):
            ve.wait_ge(tsm, rk_ts + 3)
            ve.tensor_copy(off8[:, 0, :], ps_t[:, 0:8])
            ve.tensor_copy(off8[:, 1, :], ps_m[:, 0, 0:8])
            ve.tensor_copy(tot8[:, :], ps_m[:, 1, 0:8])
            ve.drain()
            ve.tensor_scalar(zrow_d[:, :], pio_s[:, :], float(NSLOT), None, OP.add)
            ve.tensor_scalar(zrow_g[:, :], pio_s[:, :], float(NCORE * NSLOT), None,
                             OP.add)
            ve.drain()
            ve.tensor_tensor_scan(tio16[:, :], onesr_s[:, :], onesr_s[:, :], 0.0,
                                  OP.add, OP.mult)
            ve.drain()
            for j in range(2):
                ev = e1v if j == 0 else e2v
                for e in range(NE):
                    ve.tensor_scalar(asc[:, :], ev[:, :], float(e), None, OP.is_equal)
                    ve.drain()
                    ve.tensor_mul(asc[:, :], asc[:, :], vld[:, j, :])
                    ve.drain()
                    ve.tensor_tensor_scan(asc2[:, :], asc[:, :], onesr_s[:, :], 0.0, OP.add, OP.mult)
                    ve.drain()
                    ve.tensor_sub(asc2[:, :], asc2[:, :], asc[:, :])
                    ve.drain()
                    ve.tensor_scalar(asc2[:, :], asc2[:, :], off8[:, j, e:e + 1], None,
                                     OP.add)
                    ve.drain()
                    if j == 1:
                        ve.tensor_scalar(asc2[:, :], asc2[:, :], tot8[:, e:e + 1], None,
                                         OP.add)
                        ve.drain()
                    ve.tensor_mul(asc2[:, :], asc2[:, :], asc[:, :])
                    ve.drain()
                    ve.tensor_add(rnk[:, j, :], rnk[:, j, :], asc2[:, :])
                    ve.drain()
            for j in range(2):
                ev = e1v if j == 0 else e2v
                ve.tensor_scalar(asc[:, :], rnk[:, j, :], float(PAD), None, OP.is_lt)
                ve.drain()
                ve.tensor_mul(asc[:, :], asc[:, :], vld[:, j, :])
                ve.drain()
                ve.tensor_scalar(zt16[:, :], tio16[:, :], float(j * NT - 1), None,
                                 OP.add)
                ve.drain()
                ve.tensor_scalar(zt16[:, :], zt16[:, :], 128.0, None, OP.mult)
                ve.drain()
                ve.tensor_scalar(zt16[:, :], zt16[:, :], zrow_d[:, 0:1], None, OP.add)
                ve.drain()
                ve.tensor_scalar(didxf[:, j, :], ev[:, :], float(PAD), None, OP.mult)
                ve.drain()
                ve.tensor_add(didxf[:, j, :], didxf[:, j, :], rnk[:, j, :])
                ve.drain()
                ve.tensor_sub(asc2[:, :], didxf[:, j, :], zt16[:, :])
                ve.drain()
                ve.tensor_mul(asc2[:, :], asc2[:, :], asc[:, :])
                ve.drain()
                ve.tensor_add(asc2[:, :], asc2[:, :], zt16[:, :])
                ve.drain()
                ve.tensor_copy(didx[:, j, :], asc2[:, :])
                ve.drain()
                ve.tensor_scalar(asc3[:, :], ev[:, :], float(NSLOT), None, OP.mult)
                ve.drain()
                ve.tensor_add(asc3[:, :], asc3[:, :], rnk[:, j, :])
                ve.drain()
                ve.tensor_scalar(asc3[:, :], asc3[:, :], cv[:, 1:2], None, OP.add)
                ve.drain()
                ve.tensor_scalar(asc2[:, :], asc3[:, :], zrow_g[:, 0:1], None,
                                 OP.subtract)
                ve.drain()
                ve.tensor_mul(asc2[:, :], asc2[:, :], asc[:, :])
                ve.drain()
                ve.tensor_scalar(asc2[:, :], asc2[:, :], zrow_g[:, 0:1], None, OP.add)
                ve.drain()
                ve.tensor_copy(gidx[:, j, :], asc2[:, :])
                ve.drain()
            ve.tensor_add(fidxf_s[:, :], fii_s[:, :], fir_s[:, :])
            ve.drain()
            ve.tensor_scalar(fidxf_s[:, :], fidxf_s[:, :], cv[:, 1:2], None, OP.add)
            ve.drain()
            ve.tensor_copy(fidx_s[:, :], fidxf_s[:, :])
            ve.drain()
            ve.engine_nop().then_inc(vs, 1)
        C["vs"] += 1

        @block.gpsimd
        def _(g):
            g.wait_ge(vs, C["vs"])
            for j in range(2):
                for t in range(NT):
                    g.indirect_dma_start(
                        out=disp_loc[:, :],
                        out_offset=IndirectOffsetOnAxis(ap=didx[:, j, t:t + 1], axis=0),
                        in_=x1bf[:, t, :], in_offset=None,
                        bounds_check=NSLOT + 32 * 128 - 1,
                        oob_is_err=False).then_inc(dg, 16)
                    _rfree()
            g.wait_ge(dg, C["dg"] + 32 * 16)
            g.collective_compute("AllGather", OP.bypass,
                                 replica_groups=[list(range(NCORE))],
                                 ins=[disp_loc[0:NSLOT, :]], outs=[disp_all[:, :]]
                                 ).then_inc(cc, 1)
        C["dg"] += 32 * 16
        C["cc"] += 1
        cc_disp = C["cc"]

        @block.gpsimd
        def _(g):
            g.wait_ge(vs, C["vs"])
            for hc in range(8):
                g.dma_start(out=w1s[:, :, hc * 512:(hc + 1) * 512],
                            in_=w1i[:, hc * 512:(hc + 1) * 512].rearrange(
                                "(dc p) h -> p dc h", p=128)[:, :, :]).then_inc(dg, 16)
            for fc in range(8):
                g.dma_start(out=w2s[:, fc * 4:(fc + 1) * 4, :],
                            in_=w2i[fc * 512:(fc + 1) * 512, :].rearrange(
                                "(hc p) f -> p hc f", p=128)[:, :, :]).then_inc(dg, 16)
        C["dg"] += 16 * 16

        @block.sync
        def _(sy):
            sy.wait_ge(cc, cc_disp)
            sy.dma_start(out=b2s[:, :], in_=b2i[:, :]).then_inc(sd, 16)
        C["sd"] += 16
        f_sd = C["sd"]

        f_dg = C["dg"]
        f_ts, f_s2, f_vs = C["ts"], C["s2"], C["vs"]

        @block.gpsimd
        def _(g):
            g.wait_ge(cc, cc_disp)
            for sgi in range(NSG):
                for st2 in range(2):
                    st = sgi * 2 + st2
                    if sgi >= 2 and st2 == 0:
                        g.wait_ge(tsm, f_ts + (sgi - 1) * 52)
                    g.indirect_dma_start(
                        out=xdisp[:, sgi % 2, st2, :], out_offset=None,
                        in_=disp_all[:, :],
                        in_offset=IndirectOffsetOnAxis(ap=fidx_s[:, st:st + 1], axis=0),
                        bounds_check=NCORE * NSLOT - 1, oob_is_err=False
                        ).then_inc(dgx2[sgi % 2], 16)
                    _rfree()
                if sgi >= 1:
                    for ss in range(2):
                        t = (sgi - 1) * 2 + ss
                        g.wait_ge(vs, f_vs + (sgi - 1) * 18 + 16 + ss + 1)
                        r0 = (sgi - 1) * SG + ss * 128
                        g.dma_start(out=ret_loc[r0:r0 + 128, :], in_=orowb[:, :]
                                    ).then_inc(dro, 16)

        @block.tensor
        def _(te):
            te.wait_ge(dg, f_dg)
            for sgi in range(NSG):
                te.wait_ge(dgx2[sgi % 2], (sgi // 2 + 1) * 32)
                for st2 in range(2):
                    for dc in range(8):
                        i = st2 * 8 + dc
                        if i >= 2:
                            te.wait_ge(vs, f_vs + sgi * 18 + i - 1)
                        te.transpose(ps_pt[:, (i % 2) * 128:(i % 2) * 128 + 128],
                                     xdisp[:, sgi % 2, st2, dc * 128:(dc + 1) * 128],
                                     idn_s[:, :]).then_inc(tsm, 1)
                te.wait_ge(vs, f_vs + sgi * 18 + 16)
                for hc in range(32):
                    if hc >= 2:
                        te.wait_ge(s2, f_s2 + sgi * 36 + hc - 1)
                    for dc in range(8):
                        _mi = te.matmul(ps_m[:, hc % 2, 0:SG],
                                        w1s[:, dc, hc * 128:(hc + 1) * 128],
                                        xdT[:, dc, :], start=(dc == 0), stop=(dc == 7))
                    _mi.then_inc(tsm, 1)
                te.wait_ge(s2, f_s2 + sgi * 36 + 32)
                for ss in range(2):
                    if ss == 1:
                        te.wait_ge(s2, f_s2 + sgi * 36 + 34)
                    for fh in range(2):
                        for hc in range(32):
                            _mi = te.matmul(ps_pv[:, fh, :],
                                            hsb[:, hc, ss * 128:(ss + 1) * 128],
                                            w2s[:, hc, fh * 512:(fh + 1) * 512],
                                            start=(hc == 0), stop=(hc == 31))
                        _mi.then_inc(tsm, 1)
        C["ts"] += NSG * 52

        @block.scalar
        def _(sc):
            for sgi in range(NSG):
                for hc in range(32):
                    sc.wait_ge(tsm, f_ts + sgi * 52 + 16 + hc + 1)
                    if hc == 0 and sgi >= 1:
                        sc.wait_ge(tsm, f_ts + sgi * 52)
                    sc.activation(hsb[:, hc, :], ps_m[:, hc % 2, 0:SG], ACT.Silu,
                                  bias=b1s[:, hc:hc + 1], scale=1.0).then_inc(s2, 1)
                for ss in range(2):
                    t = sgi * 2 + ss
                    if t >= 1:
                        if ss == 1:
                            sc.wait_ge(vs, f_vs + sgi * 18 + 17)
                        else:
                            sc.wait_ge(vs, f_vs + sgi * 18)
                    for fh in range(2):
                        sc.wait_ge(tsm, f_ts + sgi * 52 + 48 + 2 * ss + fh + 1)
                        sc.copy(orow[:, fh * 512:(fh + 1) * 512], ps_pv[:, fh, :]
                                ).then_inc(s2, 1)
        C["s2"] += NSG * 36

        @block.vector
        def _(ve):
            ve.wait_ge(sd, f_sd)
            for sgi in range(NSG):
                for st2 in range(2):
                    for dc in range(8):
                        i = st2 * 8 + dc
                        ve.wait_ge(tsm, f_ts + sgi * 52 + i + 1)
                        ve.tensor_copy(xdT[:, dc, st2 * 128:(st2 + 1) * 128],
                                       ps_pt[:, (i % 2) * 128:(i % 2) * 128 + 128])
                        ve.drain()
                        ve.engine_nop().then_inc(vs, 1)
                for ss in range(2):
                    t = sgi * 2 + ss
                    ve.wait_ge(s2, f_s2 + sgi * 36 + 32 + 2 * ss + 2)
                    if t >= 1:
                        ve.wait_ge(dro, t * 16)
                    ve.tensor_add(orowb[:, :], orow[:, :], b2s[:, :])
                    ve.drain()
                    ve.engine_nop().then_inc(vs, 1)
        C["vs"] += NSG * 18

        @block.gpsimd
        def _(g):
            for ss in range(2):
                t = (NSG - 1) * 2 + ss
                g.wait_ge(vs, f_vs + (NSG - 1) * 18 + 16 + ss + 1)
                r0 = (NSG - 1) * SG + ss * 128
                g.dma_start(out=ret_loc[r0:r0 + 128, :], in_=orowb[:, :]
                            ).then_inc(dro, 16)
            g.wait_ge(dro, NSG * 2 * 16)
            g.collective_compute("AllGather", OP.bypass,
                                 replica_groups=[list(range(NCORE))],
                                 ins=[ret_loc[:, :]], outs=[ret_all[0:NCORE * NSLOT, :]]
                                 ).then_inc(cc, 1)
        C["cc"] += 1
        cc_ret = C["cc"]

        g_vs, g_s2, g_sd = C["vs"], C["s2"], C["sd"]

        @block.sync
        def _(sy):
            sy.wait_ge(cc, cc_ret)
            sy.wait_ge(dgx, NT * 16)
            for t in range(NT):
                if t >= 1:
                    sy.wait_ge(vs, g_vs + t * 2)
                sy.dma_start(out=x1rl[:, :] if t % 2 == 0 else qres_b[:, 0, :],
                             in_=x1_spill[t, :, :]).then_inc(sd, 16)
        C["sd"] += NT * 16

        @block.scalar
        def _(sc):
            for t in range(NT):
                sc.wait_ge(vs, g_vs + t * 2 + 1)
                sc.activation(orow[:, :], yrow[:, :], ACT.Square, bias=0.0, scale=1.0,
                              accum_out=stat[:, 1:2])
                sc.drain()
                sc.activation(stat[:, 1:2], stat[:, 1:2], ACT.Sqrt, bias=epss[:, 0:1],
                              scale=1.0 / D).then_inc(s2, 1)
        C["s2"] += NT

        @block.vector
        def _(ve):
            for t in range(NT):
                ve.wait_ge(dgg if t % 2 == 0 else dgg2,
                           (t // 2 + 1) * 32)
                ve.wait_ge(sd, g_sd + (t + 1) * 16)
                if t >= 1:
                    ve.wait_ge(dgo, t * 16)
                xv = x1rl[:, :] if t % 2 == 0 else qres_b[:, 0, :]
                ve.tensor_scalar(yrow[:, :], ga0[:, t % 2, :], g1v[:, t:t + 1], None,
                                 OP.mult)
                ve.drain()
                ve.scalar_tensor_tensor(yrow[:, :], ga1[:, t % 2, :], g2v[:, t:t + 1],
                                        yrow[:, :], OP.mult, OP.add)
                ve.drain()
                ve.tensor_add(yrow[:, :], yrow[:, :], xv)
                ve.drain()
                ve.tensor_reduce(stat[:, 0:1], yrow[:, :], AXX, OP.add)
                ve.drain()
                ve.tensor_scalar(stat[:, 0:1], stat[:, 0:1], 1.0 / D, None, OP.mult)
                ve.drain()
                ve.tensor_scalar(yrow[:, :], yrow[:, :], stat[:, 0:1], None,
                                 OP.subtract)
                ve.drain()
                ve.engine_nop().then_inc(vs, 1)
                ve.wait_ge(s2, g_s2 + t + 1)
                ve.reciprocal(stat[:, 1:2], stat[:, 1:2])
                ve.drain()
                ve.tensor_scalar(yrow[:, :], yrow[:, :], stat[:, 1:2], None, OP.mult)
                ve.drain()
                ve.engine_nop().then_inc(vs, 1)
        C["vs"] += NT * 2

        @block.gpsimd
        def _(g):
            g.wait_ge(cc, cc_ret)
            for t in range(NT):
                if t >= 2:
                    g.wait_ge(vs, g_vs + (t - 1) * 2)
                g.indirect_dma_start(
                    out=ga0[:, t % 2, :], out_offset=None, in_=ret_all[:, :],
                    in_offset=IndirectOffsetOnAxis(ap=gidx[:, 0, t:t + 1], axis=0),
                    bounds_check=NCORE * NSLOT + 127, oob_is_err=False
                    ).then_inc(dgg if t % 2 == 0 else dgg2, 16)
                _rfree()
                g.indirect_dma_start(
                    out=ga1[:, t % 2, :], out_offset=None, in_=ret_all[:, :],
                    in_offset=IndirectOffsetOnAxis(ap=gidx[:, 1, t:t + 1], axis=0),
                    bounds_check=NCORE * NSLOT + 127, oob_is_err=False
                    ).then_inc(dgg if t % 2 == 0 else dgg2, 16)
                _rfree()
                if t >= 1:
                    g.wait_ge(vs, g_vs + t * 2)
                    g.dma_start(out=out[(t - 1) // 8, ((t - 1) % 8) * 128:
                                        ((t - 1) % 8) * 128 + 128, :],
                                in_=yrow[:, :]).then_inc(dgo, 16)
            g.wait_ge(vs, g_vs + NT * 2)
            g.dma_start(out=out[(NT - 1) // 8, ((NT - 1) % 8) * 128:
                                ((NT - 1) % 8) * 128 + 128, :],
                        in_=yrow[:, :]).then_inc(dgo, 16)
            g.wait_ge(dgo, NT * 16)

    return nc
    return nc


_NC_CACHE = [None]


def kernel(L0_attn_q, L0_attn_wq, L0_attn_wk, L0_attn_wv, L0_attn_wo,
           L0_ffn_Wr, L0_ffn_br, L0_ffn_W1, L0_ffn_b1, L0_ffn_W2, L0_ffn_b2):
    q = np.ascontiguousarray(np.asarray(L0_attn_q, dtype=np.float32).reshape(S, D))
    q3 = q.reshape(KI, 16, D)
    wqh = np.ascontiguousarray(np.asarray(L0_attn_wq, np.float32))
    wkh = np.ascontiguousarray(np.asarray(L0_attn_wk, np.float32))
    wvh = np.ascontiguousarray(np.asarray(L0_attn_wv, np.float32))
    woh = np.ascontiguousarray(np.asarray(L0_attn_wo, np.float32))
    wrh = np.ascontiguousarray(np.asarray(L0_ffn_Wr, np.float32))
    brh = np.asarray(L0_ffn_br, np.float32)
    w1h = np.asarray(L0_ffn_W1, np.float32)
    b1h = np.asarray(L0_ffn_b1, np.float32)
    w2h = np.asarray(L0_ffn_W2, np.float32)
    b2h = np.asarray(L0_ffn_b2, np.float32)

    if _NC_CACHE[0] is None:
        _NC_CACHE[0] = build_nc()
    nc = _NC_CACHE[0]

    brr = np.ascontiguousarray(np.broadcast_to(brh, (128, NE)))
    in_maps = []
    for c in range(NCORE):
        b1c = np.ascontiguousarray(b1h[c].reshape(HID // 128, 128).T)
        in_maps.append({
            "q_own": q[SOWN * c:SOWN * (c + 1)],
            "q_res": np.ascontiguousarray(q3[:, 2 * c:2 * c + 2, :].transpose(1, 0, 2)),
            "wq": wqh, "wk": wkh, "wv": wvh, "wo": woh,
            "wr": wrh, "brr": brr,
            "w1i": np.ascontiguousarray(w1h[c]),
            "b1i": b1c,
            "w2i": np.ascontiguousarray(w2h[c]),
            "b2i": np.ascontiguousarray(np.broadcast_to(b2h[c], (128, D))),
            "cvec": np.ascontiguousarray(
                np.broadcast_to(np.array([c, PAD * c, NSLOT * c, 0], np.float32),
                                (128, 4))),
        })
    res = run_bass_kernel_spmd(nc, in_maps, core_ids=list(range(NCORE)))
    outp = np.empty((KI, 16, D), np.float32)
    for c in range(NCORE):
        outp[:, 2 * c:2 * c + 2, :] = np.asarray(res.results[c]["out"]).reshape(2, KI, D).transpose(1, 0, 2)
    return outp.reshape(1, S, D)
